# version 12
# baseline (speedup 1.0000x reference)
import math
import numpy as np
import ml_dtypes
import concourse.bass as bass
import concourse.mybir as mybir
from concourse.bass_utils import run_bass_kernel_spmd

F32 = mybir.dt.float32
BF16 = mybir.dt.bfloat16
FP16 = mybir.dt.float16
AF = mybir.ActivationFunctionType
ALU = mybir.AluOpType

NCORES = 8
T = 8192
D = 2048
TPC = T // NCORES
NCH = D // 128
INTER = 8192
NI = INTER // 128
NHEAD = 16
NKV = 4
HD = 128
HALO = 512
SLOPES = [2.0 ** (-0.5 * (g + 1)) for g in range(NKV)]
WINDOWS = [int(math.ceil(100.0 / s)) for s in SLOPES]
EPS = 1e-5
QSCALE = 1.0 / math.sqrt(HD)
KILL = -1e30

ENGS = ("tensor", "vector", "scalar", "gpsimd", "sync")
SEMS = ("wg0", "wg1", "wd0", "wd1", "gx", "gb", "sdma", "sdx0", "sdx1", "mm", "vv", "ss", "cc")
SLOT_SEM = ("wg0", "wg1", "wd0", "wd1")

_STAGE = "full"


def _chunks(g, qt):
    q0 = HALO + 512 * qt
    nlo = max(0, (q0 - WINDOWS[g]) // 128)
    nhi = q0 // 128 + 3
    return list(range(nlo, nhi + 1))


class Prog:

    def __init__(self):
        self.streams = {k: [] for k in ENGS}
        self.cnt = {s: 0 for s in SEMS}
        self.lastwait = {k: {} for k in ENGS}

    OWN = {"vector": "vv", "scalar": "ss"}

    def emit(self, eng, fn, waits=(), inc=None):
        wl = []
        own = self.OWN.get(eng)
        allw = list(waits)
        if own is not None:
            assert inc == (own, 1), (eng, inc)
            if self.cnt[own] > 0:
                allw.append((own, self.cnt[own]))
        for w in allw:
            if w is None:
                continue
            s, v = w
            if v and v > 0 and self.lastwait[eng].get(s, -1) < v:
                wl.append((s, v))
                self.lastwait[eng][s] = v
        val = None
        if inc is not None:
            self.cnt[inc[0]] += inc[1]
            val = self.cnt[inc[0]]
        self.streams[eng].append((tuple(wl), fn, inc))
        return val

    def replay(self, name, eng, sems):
        for wl, fn, inc in self.streams[name]:
            for s, v in wl:
                eng.wait_ge(sems[s], v)
            instr = fn(eng) if fn is not None else None
            if inc is not None:
                if instr is None:
                    instr = eng.nop()
                instr.then_inc(sems[inc[0]], inc[1])


def build(stage):
    nc = bass.Bass("TRN2", target_bir_lowering=False)

    xt = nc.dram_tensor("xt", [128, NCH, TPC], F32, kind="ExternalInput").ap()
    wgt = nc.dram_tensor("wgt", [NI, 128, NCH, 2, 128], F32, kind="ExternalInput").ap()
    wdt = nc.dram_tensor("wdt", [NCH, 128, NI, 128], F32, kind="ExternalInput").ap()
    wq = nc.dram_tensor("wq", [NHEAD, 128, NCH, 128], F32, kind="ExternalInput").ap()
    wk = nc.dram_tensor("wk", [128, NCH, NKV, 128], F32, kind="ExternalInput").ap()
    wv = nc.dram_tensor("wv", [128, NCH, 512], F32, kind="ExternalInput").ap()
    wo = nc.dram_tensor("wo", [NCH, 128, NHEAD, 128], F32, kind="ExternalInput").ap()
    alibi = nc.dram_tensor("alibi", [128, 1408], FP16, kind="ExternalInput").ap()
    hb = nc.dram_tensor("hb", [128, 16], F32, kind="ExternalInput").ap()
    out = nc.dram_tensor("out", [128, NCH, TPC], F32, kind="ExternalOutput").ap()
    cc_in = nc.dram_tensor("cc_in", [2, 512, 512], BF16).ap()
    cc_out = nc.dram_tensor("cc_out", [16, 512, 512], BF16).ap()

    from contextlib import ExitStack
    with ExitStack() as ctx:
        e = ctx.enter_context
        x_sb = e(nc.sbuf_tensor("x_sb", [128, NCH * TPC], F32))
        xn_sb = e(nc.sbuf_tensor("xn_sb", [128, NCH * TPC], BF16))
        uni = e(nc.sbuf_tensor("uni", [128, 32768], BF16))
        wgu = e(nc.sbuf_tensor("wgu", [128, 8192], BF16))
        wdb = e(nc.sbuf_tensor("wdb", [128, 8192], BF16))
        al_sb = e(nc.sbuf_tensor("al_sb", [128, 1408], FP16))
        hb_sb = e(nc.sbuf_tensor("hb_sb", [128, 16], F32))
        sgb = e(nc.sbuf_tensor("sgb", [128, 1024], BF16))
        stats = e(nc.sbuf_tensor("stats", [1, 2048], F32))
        tmp512 = e(nc.sbuf_tensor("tmp512", [128, 512], F32))
        ones_bf = e(nc.sbuf_tensor("ones_bf", [128, 1], BF16))
        ones_f32 = e(nc.sbuf_tensor("ones_f32", [1, 128], F32))

        pm = [e(nc.psum_tensor(f"pm{i}", [128, 512], F32)) for i in range(4)]
        pmA = [e(nc.psum_tensor(f"pmA{i}", [128, 512], F32)) for i in range(2)]
        plb = [e(nc.psum_tensor(f"pl{i}", [128, 512], F32)) for i in range(2)]
        PB = pm + pmA + plb

        sems = {s: e(nc.semaphore(s)) for s in SEMS}
        block = e(nc.Block())

        P = Prog()
        pst = {id(t): {"wr": None, "rd": None} for t in PB}

        def xnv(c, lo, n):
            return xn_sb[:, :].rearrange("p (c t) -> p c t", c=NCH)[:, c, lo:lo + n]

        def xv(c, lo, n):
            return x_sb[:, :].rearrange("p (c t) -> p c t", c=NCH)[:, c, lo:lo + n]

        def mm_group(ps, tiles, waits=(), rows=128, tile_waits=None):
            st = pst[id(ps)]
            w = list(waits)
            if st["rd"] is not None:
                w.append(st["rd"])
            n = len(tiles)
            val = None
            for k, (lf, rf) in enumerate(tiles):
                first, last = (k == 0), (k == n - 1)
                wk = list(w) if k == 0 else []
                if tile_waits is not None and tile_waits[k] is not None:
                    wk.append(tile_waits[k])
                def fn(eng, lf=lf, rf=rf, first=first, last=last, ps=ps, rows=rows):
                    return eng.matmul(ps[0:rows, :], lf(), rf(), start=first, stop=last)
                val = P.emit("tensor", fn, waits=wk,
                             inc=(("mm", 1) if last else None))
            st["wr"] = ("mm", val)
            return val

        def psum_read(ps, sem, val):
            pst[id(ps)]["rd"] = (sem, val)

        KOFF, VOFF, QYOFF, SPOFF = 0, NKV * 1536, 12288, 28672

        def Kap(kv, t0, n):
            return uni[:, KOFF + kv * 1536 + t0: KOFF + kv * 1536 + t0 + n]

        def Vap(n, d0, nd):
            return uni[:, VOFF + n * 512 + d0: VOFF + n * 512 + d0 + nd]

        def QYap(h, qt):
            o = QYOFF + h * 1024 + qt * 512
            return uni[:, o:o + 512]

        def Hap(tau, kk):
            o = (tau * 32 + kk) * 512
            return uni[:, o:o + 512]

        def Sap(slot):
            return uni[:, SPOFF + slot * 512: SPOFF + (slot + 1) * 512]

        def stat(i):
            return stats[:, i * 512:(i + 1) * 512]

        def slot_ap(slot, off, n):
            buf = wgu if slot < 2 else wdb
            base = (slot % 2) * 4096
            return buf[:, base + off: base + off + n]

        ones_rdy = P.emit("vector", lambda v: v.memset(ones_bf[:, :], 1.0), inc=("vv", 1))
        P.emit("vector", lambda v: v.memset(ones_f32[:, :], 1.0), inc=("vv", 1))
        for tau in range(2):
            def fn(sy, tau=tau):
                return sy.dma_start(
                    out=x_sb[:, :].rearrange("p (c t) -> p c t", c=NCH)[:, :, tau * 512:(tau + 1) * 512],
                    in_=xt[:, :, tau * 512:(tau + 1) * 512])
            P.emit("sync", fn, inc=("sdx%d" % tau, 16))
        P.emit("sync", lambda sy: sy.dma_start(out=al_sb[:, :], in_=alibi), inc=("sdma", 16))
        P.emit("sync", lambda sy: sy.dma_start(out=hb_sb[:, :], in_=hb), inc=("sdma", 16))
        LOADS = 32

        slot_cons = [None, None, None, None]
        sg_cons = [None, None]
        sp_cons = [None, None]
        last_stat_mm = [None]

        def load_wgt(i, slot):
            wsem = SLOT_SEM[slot]
            w = [slot_cons[slot]]
            def fdma(gp, i=i, slot=slot):
                return gp.dma_start(
                    out=slot_ap(slot, 0, 4096).rearrange(
                        "p (j g o) -> p j g o", j=NCH, g=2),
                    in_=wgt[i])
            return P.emit("gpsimd", fdma, waits=w, inc=(wsem, 16))


        def ln_start():
            return {"n": [0, 0], "mmv": {}, "vp": None, "sl": 0,
                    "pb": {(0, 'm'): PB[4], (0, 'q'): PB[5], (1, 'm'): PB[6], (1, 'q'): PB[7]}}

        def ln_cast(tau, c, waits):
            def fc(gp, tau=tau, c=c):
                xr = x_sb[:, :].rearrange("p (c t) -> p c t", c=NCH)
                nr = xn_sb[:, :].rearrange("p (c t) -> p c t", c=NCH)
                if c is None:
                    return gp.dma_start(out=nr[:, :, tau * 512:(tau + 1) * 512],
                                        in_=xr[:, :, tau * 512:(tau + 1) * 512])
                return gp.dma_start(out=nr[:, c, tau * 512:(tau + 1) * 512],
                                    in_=xr[:, c, tau * 512:(tau + 1) * 512])
            return P.emit("gpsimd", fc, waits=waits, inc=("gx", 16))

        def ln_chunk(L, tau, c, cast_val):
            if L["vp"] is None:
                L["vp"] = [sg_cons[0], sg_cons[1]]
            pM, pQ = L["pb"][(tau, 'm')], L["pb"][(tau, 'q')]
            k = L["n"][tau]
            first, last = k == 0, k == NCH - 1
            wm = [("gx", cast_val), ("vv", ones_rdy)]
            if first:
                wm.append(pst[id(pM)]["rd"])
            def fm(eng, tau=tau, c=c, pM=pM, first=first, last=last):
                return eng.matmul(pM[0:1, :], ones_bf[:, :], xnv(c, tau * 512, 512),
                                  start=first, stop=last)
            mM = P.emit("tensor", fm, waits=wm, inc=("mm", 1))
            sl = L["sl"]
            L["sl"] ^= 1
            sq = P.emit("scalar",
                        lambda s, c=c, sl=sl, tau=tau: s.activation(
                            sgb[:, sl * 512:(sl + 1) * 512], xnv(c, tau * 512, 512), AF.Square),
                        waits=[("gx", cast_val), L["vp"][sl]], inc=("ss", 1))
            wq = [("ss", sq)]
            if first:
                wq.append(pst[id(pQ)]["rd"])
            def fq(eng, sl=sl, pQ=pQ, first=first, last=last):
                return eng.matmul(pQ[0:1, :], ones_bf[:, :], sgb[:, sl * 512:(sl + 1) * 512],
                                  start=first, stop=last)
            mQ = P.emit("tensor", fq, waits=wq, inc=("mm", 1))
            L["vp"][sl] = ("mm", mQ)
            L["n"][tau] += 1
            if last:
                L["mmv"][tau] = (mM, mQ)
                pst[id(pM)]["wr"] = ("mm", mM)
                pst[id(pQ)]["wr"] = ("mm", mQ)

        def ln_finale(L, stat_guard=None):
            inv_d = 1.0 / D
            binfo = {}
            for tau in range(2):
                so = 0
                pM, pQ = L["pb"][(tau, 'm')], L["pb"][(tau, 'q')]
                mM, mQ = L["mmv"][tau]
                sgw = [stat_guard] if stat_guard is not None else []
                if tau == 1:
                    sgw.append(("mm", binfo[0]))
                P.emit("vector", lambda v, pM=pM, so=so: v.tensor_scalar(
                    stat(so), pM[0:1, :], inv_d, None, ALU.mult),
                       waits=[("mm", mM)] + sgw, inc=("vv", 1))
                v2 = P.emit("vector", lambda v, pQ=pQ, so=so: v.tensor_scalar(
                    stat(so + 1), pQ[0:1, :], inv_d, EPS, ALU.mult, ALU.add),
                            waits=[("mm", mQ)], inc=("vv", 1))
                P.emit("vector", lambda v, so=so: v.tensor_tensor(
                    stat(so + 2), stat(so), stat(so), ALU.mult), inc=("vv", 1))
                v4 = P.emit("vector", lambda v, so=so: v.tensor_tensor(
                    stat(so + 1), stat(so + 1), stat(so + 2), ALU.subtract), inc=("vv", 1))
                s1 = P.emit("scalar", lambda s, so=so: s.activation(
                    stat(so + 2), stat(so + 1), AF.Sqrt), waits=[("vv", v4)], inc=("ss", 1))
                P.emit("vector", lambda v, so=so: v.reciprocal(stat(so + 3), stat(so + 2)),
                       waits=[("ss", s1)], inc=("vv", 1))
                P.emit("vector", lambda v, so=so: v.tensor_tensor(
                    stat(so + 2), stat(so + 3), stat(so + 3), ALU.mult), inc=("vv", 1))
                P.emit("vector", lambda v, so=so: v.tensor_tensor(
                    stat(so + 2), stat(so + 2), stat(so + 1), ALU.mult), inc=("vv", 1))
                P.emit("vector", lambda v, so=so: v.tensor_scalar(
                    stat(so + 2), stat(so + 2), -0.5, 1.5, ALU.mult, ALU.add), inc=("vv", 1))
                v9 = P.emit("vector", lambda v, so=so: v.tensor_tensor(
                    stat(so + 3), stat(so + 3), stat(so + 2), ALU.mult), inc=("vv", 1))
                vB = P.emit("vector", lambda v, so=so: v.scalar_tensor_tensor(
                    stat(so), stat(so), -1.0, stat(so + 3), ALU.mult, ALU.mult), inc=("vv", 1))
                psum_read(pM, "vv", vB)
                psum_read(pQ, "vv", v2)
                bA = mm_group(pM, [((lambda: ones_f32[:, :]), (lambda so=so: stat(so + 3)))],
                              waits=[("vv", v9)])
                bB = mm_group(pQ, [((lambda: ones_f32[:, :]), (lambda so=so: stat(so)))],
                              waits=[("vv", vB)])
                binfo[tau] = bB
                last_stat_mm[0] = ("mm", bB)
            xn_vv = [[None] * NCH, [None] * NCH]
            for c in range(NCH):
                for tau in range(2):
                    pM, pQ = L["pb"][(tau, 'm')], L["pb"][(tau, 'q')]
                    wts = [("mm", binfo[tau])] if c == 0 else ()
                    P.emit("vector",
                           lambda v, c=c, tau=tau, pM=pM: v.tensor_tensor(
                               tmp512[:, :], xv(c, tau * 512, 512), pM[:, :], ALU.mult),
                           waits=wts, inc=("vv", 1))
                    xn_vv[tau][c] = P.emit("vector",
                                           lambda v, c=c, tau=tau, pQ=pQ: v.tensor_tensor(
                                               xnv(c, tau * 512, 512), tmp512[:, :], pQ[:, :], ALU.add),
                                           inc=("vv", 1))
            for tau in range(2):
                psum_read(L["pb"][(tau, 'm')], "vv", xn_vv[tau][-1])
                psum_read(L["pb"][(tau, 'q')], "vv", xn_vv[tau][-1])
            sg_cons[0], sg_cons[1] = L["vp"][0], L["vp"][1]
            return xn_vv

        def emit_ffn(xn_vv, preload=None, h_guard=None, store=False, ln_job=None):
            vres = [None, None]
            m_down = None
            guard = h_guard
            ln_pend = []
            for half in range(2):
                hv_last = [None, None]
                for ii in range(32):
                    i = half * 32 + ii
                    slot = i % 4 if ii < 28 else i % 2
                    if preload is not None and i in preload:
                        ld = preload.pop(i)
                    else:
                        ld = load_wgt(i, slot)
                    wsem = SLOT_SEM[slot]
                    B = PB[0:4] if ii % 2 == 0 else PB[4:8]
                    grp = {}
                    for j in range(NCH):
                        for gi in range(2):
                            for tau in range(2):
                                ps = B[gi * 2 + tau]
                                first, last = j == 0, j == NCH - 1
                                w = [("vv", xn_vv[tau][j])]
                                if first:
                                    w += [(wsem, ld), pst[id(ps)]["rd"]]
                                def fn(eng, slot=slot, j=j, gi=gi, tau=tau, ps=ps,
                                       first=first, last=last):
                                    return eng.matmul(ps[:, :], slot_ap(slot, j * 256 + gi * 128, 128),
                                                      xnv(j, tau * 512, 512), start=first, stop=last)
                                val = P.emit("tensor", fn, waits=w,
                                             inc=(("mm", 1) if last else None))
                                if last:
                                    grp[(gi, tau)] = val
                                    pst[id(ps)]["wr"] = ("mm", val)
                    slot_cons[slot] = ("mm", val)
                    for tau in range(2):
                        sl = tau
                        sU = P.emit("scalar",
                                    lambda s, sl=sl, pG=B[tau]: s.activation(
                                        sgb[:, sl * 512:(sl + 1) * 512], pG[:, :], AF.Silu),
                                    waits=[("mm", grp[(0, tau)]), sg_cons[sl]], inc=("ss", 1))
                        psum_read(B[tau], "ss", sU)
                        hw = [("ss", sU), ("mm", grp[(1, tau)])]
                        if guard is not None:
                            hw.append(guard)
                        hv = P.emit("vector",
                                    lambda v, tau=tau, ii=ii, sl=sl, pU=B[2 + tau]: v.scalar_tensor_tensor(
                                        Hap(tau, ii), pU[:, :], 0.5, sgb[:, sl * 512:(sl + 1) * 512],
                                        ALU.mult, ALU.mult),
                                    waits=hw, inc=("vv", 1))
                        psum_read(B[2 + tau], "vv", hv)
                        sg_cons[sl] = ("vv", hv)
                        hv_last[tau] = hv
                for o in range(NCH):
                    wslot = 2 + o % 2
                    wsem = SLOT_SEM[wslot]
                    def fdma(gp, o=o, half=half, wslot=wslot):
                        return gp.dma_start(
                            out=slot_ap(wslot, 0, 4096).rearrange("p (q c) -> p q c", q=32),
                            in_=wdt[o, :, half * 32:(half + 1) * 32, :])
                    ld = P.emit("gpsimd", fdma, waits=[slot_cons[wslot]], inc=(wsem, 16))
                    pd = (PB[0], PB[1]) if o % 2 == 0 else (PB[2], PB[3])
                    md = [None, None]
                    for kk in range(32):
                        for tau in range(2):
                            ps = pd[tau]
                            first, last = kk == 0, kk == 31
                            w = []
                            if first:
                                w = [(wsem, ld), ("vv", hv_last[tau]), pst[id(ps)]["rd"]]
                            def fn(eng, wslot=wslot, kk=kk, tau=tau, ps=ps, first=first, last=last):
                                return eng.matmul(ps[:, :], slot_ap(wslot, kk * 128, 128),
                                                  Hap(tau, kk), start=first, stop=last)
                            md[tau] = P.emit("tensor", fn, waits=w,
                                             inc=(("mm", 1) if last else None))
                            if last:
                                pst[id(ps)]["wr"] = ("mm", md[tau])
                    slot_cons[wslot] = ("mm", md[1])
                    m_down = md[1]
                    for tau in range(2):
                        vres[tau] = P.emit("vector",
                                           lambda v, o=o, tau=tau, ps=pd[tau]: v.tensor_tensor(
                                               xv(o, tau * 512, 512), xv(o, tau * 512, 512), ps[:, :], ALU.add),
                                           waits=[("mm", md[tau])], inc=("vv", 1))
                        psum_read(pd[tau], "vv", vres[tau])
                        if store and half == 1:
                            def fst(sy, o=o, tau=tau):
                                return sy.dma_start(
                                    out=out[:, o, tau * 512:(tau + 1) * 512],
                                    in_=xv(o, tau * 512, 512))
                            P.emit("sync", fst, waits=[("vv", vres[tau])], inc=("sdma", 16))
                    if ln_job is not None and half == 1:
                        ln_pend.append((o, vres[0], vres[1]))
                        if len(ln_pend) > 1:
                            po, pv0, pv1 = ln_pend.pop(0)
                            for tau, pv in ((0, pv0), (1, pv1)):
                                cv = ln_cast(tau, po, [("vv", pv)])
                                ln_chunk(ln_job, tau, po, cv)
                guard = ("mm", m_down)
            return vres, m_down, ln_pend

        L1 = ln_start()
        cast0 = ln_cast(0, None, [("sdx0", 16)])
        preload1 = {0: load_wgt(0, 0), 1: load_wgt(1, 1)}
        cast1 = ln_cast(1, None, [("sdx1", 16)])
        preload1[2] = load_wgt(2, 2)
        preload1[3] = load_wgt(3, 3)
        for c in range(NCH):
            ln_chunk(L1, 0, c, cast0)
        for c in range(NCH):
            ln_chunk(L1, 1, c, cast1)
        ln1_vv = ln_finale(L1)
        L2 = ln_start()
        ffn1, m_ffn1, ln2_pend = emit_ffn(ln1_vv, preload=preload1, ln_job=L2)

        if stage == "ffn1":
            _emit_store(P, x_sb, out, [("vv", ffn1[0]), ("vv", ffn1[1])])
            _close(P, block, sems)
            return nc

        ldk = P.emit("gpsimd",
                     lambda gp: gp.dma_start(
                         out=wgu[:, :].rearrange("p (j kv d) -> p j kv d", j=NCH, kv=NKV), in_=wk),
                     waits=[slot_cons[0], slot_cons[1]], inc=("wg0", 16))
        ldv = P.emit("gpsimd",
                     lambda gp: gp.dma_start(
                         out=wdb[:, :].rearrange("p (j d) -> p j d", j=NCH), in_=wv),
                     waits=[slot_cons[2], slot_cons[3]], inc=("wd0", 16))
        for po, pv0, pv1 in ln2_pend:
            for tau, pv in ((0, pv0), (1, pv1)):
                cv = ln_cast(tau, po, [("vv", pv)])
                ln_chunk(L2, tau, po, cv)

        ln2_vv = ln_finale(L2, stat_guard=last_stat_mm[0])

        k_evac = None
        m_last = None
        for kv in range(NKV):
            for qt in range(2):
                pK = pm[(kv * 2 + qt) % 2]
                tiles = [((lambda j=j, kv=kv: wgu[:, j * 512 + kv * 128: j * 512 + (kv + 1) * 128]),
                          (lambda j=j, qt=qt: xnv(j, qt * 512, 512))) for j in range(NCH)]
                tw = [("vv", ln2_vv[qt][j]) for j in range(NCH)]
                mK = mm_group(pK, tiles, waits=[("wg0", ldk)], tile_waits=tw)
                m_last = mK
                k_evac = P.emit("scalar",
                                lambda s, kv=kv, qt=qt, pK=pK: s.activation(
                                    Kap(kv, HALO + qt * 512, 512), pK[:, :], AF.Copy),
                                waits=[("mm", mK)], inc=("ss", 1))
                psum_read(pK, "ss", k_evac)
        slot_cons[0] = slot_cons[1] = ("mm", m_last)
        v_evac = None
        for w8 in range(8):
            pV = pm[w8 % 2]
            tiles = [((lambda j=j, w8=w8: xnv(j, w8 * 128, 128)),
                      (lambda j=j: wdb[:, j * 512:(j + 1) * 512])) for j in range(NCH)]
            tw = [("vv", ln2_vv[w8 // 4][j]) for j in range(NCH)]
            mV = mm_group(pV, tiles, waits=[("wd0", ldv)], tile_waits=tw)
            m_last = mV
            v_evac = P.emit("scalar",
                            lambda s, w8=w8, pV=pV: s.activation(Vap(4 + w8, 0, 512), pV[:, :], AF.Copy),
                            waits=[("mm", mV)], inc=("ss", 1))
            psum_read(pV, "ss", v_evac)
        slot_cons[2] = slot_cons[3] = ("mm", m_last)

        def f_bk(gp):
            return gp.dma_start(
                out=cc_in[0].rearrange("(kv p) t -> p kv t", p=128),
                in_=uni[:, KOFF:KOFF + NKV * 1536].rearrange(
                    "p (kv t) -> p kv t", kv=NKV)[:, :, 1024:1536])
        P.emit("gpsimd", f_bk, waits=[("ss", k_evac)], inc=("gb", 16))
        def f_bv(gp):
            return gp.dma_start(
                out=cc_in[1].rearrange("(w p) d -> p w d", p=128),
                in_=uni[:, VOFF + 8 * 512: VOFF + 12 * 512].rearrange("p (w d) -> p w d", w=4))
        bv = P.emit("gpsimd", f_bv, waits=[("ss", v_evac)], inc=("gb", 16))

        ccv = P.emit("gpsimd",
                     lambda gp: gp.collective_compute(
                         "AllGather", ALU.bypass, replica_groups=[list(range(NCORES))],
                         ins=[cc_in], outs=[cc_out]),
                     waits=[("gb", bv)], inc=("cc", 1))

        q_evac = {}
        for h in range(NHEAD):
            par = h % 2
            wsem = "wg%d" % par
            def fdma(gp, h=h, par=par):
                return gp.dma_start(
                    out=wgu[:, par * 4096: par * 4096 + 2048].rearrange("p (j e) -> p j e", j=NCH),
                    in_=wq[h])
            ldq = P.emit("gpsimd", fdma, waits=[slot_cons[par]], inc=(wsem, 16))
            mq = None
            for qt in range(2):
                pQ = pm[2 + (h * 2 + qt) % 2]
                tiles = [((lambda j=j, par=par: wgu[:, par * 4096 + j * 128: par * 4096 + (j + 1) * 128]),
                          (lambda j=j, qt=qt: xnv(j, qt * 512, 512))) for j in range(NCH)]
                mq = mm_group(pQ, tiles, waits=[(wsem, ldq)],
                              tile_waits=[("vv", ln2_vv[qt][j]) for j in range(NCH)])
                q_evac[(h, qt)] = P.emit("scalar",
                                         lambda s, h=h, qt=qt, pQ=pQ: s.mul(QYap(h, qt), pQ[:, :], QSCALE),
                                         waits=[("mm", mq)], inc=("ss", 1))
                psum_read(pQ, "ss", q_evac[(h, qt)])
            slot_cons[par] = ("mm", mq)

        def f_halo(gp):
            cid = nc.partition_id([gp.engine])
            for i in gp.Switch(cid, NCORES):
                r = (i - 1) % NCORES
                gp.dma_start(
                    out=uni[:, KOFF:KOFF + NKV * 1536].rearrange(
                        "p (kv t) -> p kv t", kv=NKV)[:, :, 0:512],
                    in_=cc_out[2 * r].rearrange("(kv p) t -> p kv t", p=128)
                ).then_inc(sems["gb"], 16)
                gp.dma_start(
                    out=uni[:, VOFF:VOFF + 4 * 512].rearrange("p (w d) -> p w d", w=4),
                    in_=cc_out[2 * r + 1].rearrange("(w p) d -> p w d", p=128)
                ).then_inc(sems["gb"], 16)
        P.cnt["gb"] += 32
        halo_done = P.cnt["gb"]
        P.emit("gpsimd", lambda gp: (f_halo(gp), None)[1], waits=[("cc", ccv)])

        SPS = [pm[0], pm[1], pmA[0]]
        sp_cons = [("mm", m_ffn1)] * 3
        binv = sgb[:, :].bitcast(F32)
        wo_ready_vv = None
        stat_rd = [None, None]
        pending = None
        iters = [(g, g * 4 + hh, qt) for g in range(NKV) for hh in range(4) for qt in range(2)]

        def emit_norm_recip(h, qt, pY, mY, plX, mL, sc, extra_w):
            vr_w = [("mm", mL), stat_rd[sc]] + list(extra_w)
            P.emit("vector", lambda v, plX=plX, sc=sc: v.reciprocal(stat(sc), plX[0:1, :]),
                   waits=vr_w, inc=("vv", 1))
            P.emit("vector", lambda v, plX=plX, sc=sc: v.tensor_tensor(
                stat(2 + sc), plX[0:1, :], stat(sc), ALU.mult), inc=("vv", 1))
            P.emit("vector", lambda v, sc=sc: v.tensor_scalar(
                stat(2 + sc), stat(2 + sc), -1.0, 2.0, ALU.mult, ALU.add), inc=("vv", 1))
            vr_v = P.emit("vector", lambda v, sc=sc: v.tensor_tensor(
                stat(sc), stat(sc), stat(2 + sc), ALU.mult), inc=("vv", 1))
            psum_read(plX, "vv", vr_v)
            return vr_v

        def emit_norm_rest(h, qt, pY, mY, sc, vr_v):
            pB = pmA[1]
            mB = mm_group(pB, [((lambda: ones_f32[:, :]), (lambda sc=sc: stat(sc)))],
                          waits=[("vv", vr_v)])
            stat_rd[sc] = ("mm", mB)
            vc = P.emit("scalar", lambda s, pB=pB: s.copy(binv[:, 0:512], pB[:, :]),
                        waits=[("mm", mB), ("vv", wo_ready_vv), sg_cons[0], sg_cons[1]], inc=("ss", 1))
            psum_read(pB, "ss", vc)
            vy = P.emit("vector",
                        lambda v, h=h, qt=qt, pY=pY: v.tensor_tensor(
                            QYap(h, qt), pY[:, :], binv[:, 0:512], ALU.mult),
                        waits=[("ss", vc), ("mm", mY)], inc=("vv", 1))
            psum_read(pY, "vv", vy)
            return vy

        first_norm = True
        for it, (g, h, qt) in enumerate(iters):
            slope = SLOPES[g]
            q0 = HALO + 512 * qt
            ch = _chunks(g, qt)
            ch = [n for n in ch if n >= 4] + [n for n in ch if n < 4]
            n_t = len(ch)
            pY = pm[2 + it % 2]
            plX = plb[it % 2]
            yw0 = pst[id(pY)]["rd"]
            lw0 = pst[id(plX)]["rd"]
            sE_of = [None] * n_t

            def emit_S(ci, h=h, qt=qt, g=g, slope=slope, q0=q0, ch=ch, sE_of=sE_of):
                n = ch[ci]
                pS = SPS[ci % 3]
                slot = ci % 3
                sw = [("ss", q_evac[(h, qt)]), pst[id(pS)]["rd"]]
                if n < 4:
                    sw.append(("gb", halo_done))
                def fs(eng, g=g, n=n, h=h, qt=qt, pS=pS):
                    return eng.matmul(pS[:, :], Kap(g, n * 128, 128), QYap(h, qt),
                                      start=True, stop=True)
                mS = P.emit("tensor", fs, waits=sw, inc=("mm", 1))
                pst[id(pS)]["wr"] = ("mm", mS)
                a = n * 128 - q0
                sl_lo = 384 - a
                aw = [("mm", mS), ("sdma", LOADS)]
                def fa(v, sl_lo=sl_lo, slope=slope, pS=pS):
                    return v.scalar_tensor_tensor(pS[:, :], al_sb[:, sl_lo:sl_lo + 512],
                                                  slope, pS[:, :], ALU.mult, ALU.add)
                vA = P.emit("vector", fa, waits=aw, inc=("vv", 1))
                pst[id(pS)]["wr"] = ("vv", vA)
                ew = [("vv", vA), sp_cons[slot]]
                if n < 4:
                    bias_arg = hb_sb[:, g * 4 + n: g * 4 + n + 1]
                else:
                    bias_arg = None
                def fe(s, slot=slot, pS=pS, bias_arg=bias_arg):
                    if bias_arg is None:
                        return s.activation(Sap(slot), pS[:, :], AF.Exp)
                    return s.activation(Sap(slot), pS[:, :], AF.Exp, bias=bias_arg)
                sE = P.emit("scalar", fe, waits=ew, inc=("ss", 1))
                psum_read(pS, "ss", sE)
                sE_of[ci] = sE

            if pending is not None:
                pend_vr = emit_norm_recip(*pending)
            emit_S(0)
            emit_S(1)
            emit_S(2)
            mY = mL = None
            for ci in range(n_t):
                n = ch[ci]
                slot = ci % 3
                first, last = ci == 0, ci == n_t - 1
                w_y = [("ss", sE_of[ci])] + ([yw0] if first and yw0 else [])
                if n < 4:
                    w_y.append(("gb", halo_done))
                def fy(eng, g=g, n=n, slot=slot, first=first, last=last, pY=pY):
                    return eng.matmul(pY[:, :], Vap(n, g * 128, 128), Sap(slot),
                                      start=first, stop=last)
                mY = P.emit("tensor", fy, waits=w_y, inc=(("mm", 1) if last else None))
                def fl(eng, slot=slot, first=first, last=last, plX=plX):
                    return eng.matmul(plX[0:1, :], ones_bf[:, :], Sap(slot),
                                      start=first, stop=last)
                mL = P.emit("tensor", fl, waits=([lw0] if first and lw0 else ()),
                            inc=("mm", 1))
                sp_cons[slot] = ("mm", mL)
                if ci + 3 < n_t:
                    emit_S(ci + 3)
            pst[id(pY)]["wr"] = ("mm", mY)
            pst[id(plX)]["wr"] = ("mm", mL)
            sc = it % 2
            extra_w = []
            if first_norm:
                extra_w.append(last_stat_mm[0])
                first_norm = False
            if pending is not None:
                wo_ready_vv = emit_norm_rest(pending[0], pending[1], pending[2],
                                             pending[3], pending[6], pend_vr)
            pending = (h, qt, pY, mY, plX, mL, sc, extra_w)
        pend_vr = emit_norm_recip(*pending)
        wo_ready_vv = emit_norm_rest(pending[0], pending[1], pending[2],
                                     pending[3], pending[6], pend_vr)
        pending = None

        sg_cons[0] = ("vv", wo_ready_vv)
        sg_cons[1] = ("vv", wo_ready_vv)
        L3 = ln_start()
        ln3_pend = []
        wo_done_vv = None
        m_wo = None
        for o in range(NCH):
            par = o % 2
            wsem = "wg%d" % par
            def fdma(gp, o=o, par=par):
                return gp.dma_start(
                    out=wgu[:, par * 4096: par * 4096 + 2048].rearrange("p (hh oo) -> p hh oo", hh=NHEAD),
                    in_=wo[o])
            ldo = P.emit("gpsimd", fdma, waits=[slot_cons[par]], inc=(wsem, 16))
            wov = [None, None]
            for qt in range(2):
                pO = pm[(o * 2 + qt) % 2]
                tiles = [((lambda hh=hh, par=par: wgu[:, par * 4096 + hh * 128: par * 4096 + (hh + 1) * 128]),
                          (lambda hh=hh, qt=qt: QYap(hh, qt))) for hh in range(NHEAD)]
                mO = mm_group(pO, tiles, waits=[(wsem, ldo), ("vv", wo_ready_vv)])
                m_wo = mO
                wo_done_vv = P.emit("vector",
                                    lambda v, o=o, qt=qt, pO=pO: v.tensor_tensor(
                                        xv(o, qt * 512, 512), xv(o, qt * 512, 512), pO[:, :], ALU.add),
                                    waits=[("mm", mO)], inc=("vv", 1))
                psum_read(pO, "vv", wo_done_vv)
                wov[qt] = wo_done_vv
            slot_cons[par] = ("mm", m_wo)
            ln3_pend.append((o, wov[0], wov[1]))
            if len(ln3_pend) > 1:
                po, pv0, pv1 = ln3_pend.pop(0)
                for tau, pv in ((0, pv0), (1, pv1)):
                    cv = ln_cast(tau, po, [("vv", pv)])
                    ln_chunk(L3, tau, po, cv)
        for po, pv0, pv1 in ln3_pend:
            for tau, pv in ((0, pv0), (1, pv1)):
                cv = ln_cast(tau, po, [("vv", pv)])
                ln_chunk(L3, tau, po, cv)

        if stage == "attn":
            _emit_store(P, x_sb, out, [("vv", wo_done_vv)])
            _close(P, block, sems)
            return nc

        preload2 = {i: load_wgt(i, i) for i in range(4)}

        ln3_vv = ln_finale(L3, stat_guard=("mm", m_wo))
        ffn2, m_ffn2, _ = emit_ffn(ln3_vv, preload=preload2,
                                   h_guard=("mm", m_wo), store=True)
        P.emit("sync", None, waits=[("sdma", P.cnt["sdma"])])
        _close(P, block, sems)
    return nc


def _emit_store(P, x_sb, out, waits):
    def fn(sy):
        return sy.dma_start(out=out, in_=x_sb[:, :].rearrange("p (c t) -> p c t", c=NCH))
    v = P.emit("sync", fn, waits=waits, inc=("sdma", 16))
    P.emit("sync", None, waits=[("sdma", v)])


def _close(P, block, sems):
    @block.tensor
    def _(t):
        P.replay("tensor", t, sems)

    @block.vector
    def _(v):
        P.replay("vector", v, sems)

    @block.scalar
    def _(s):
        P.replay("scalar", s, sems)

    @block.gpsimd
    def _(gp):
        P.replay("gpsimd", gp, sems)

    @block.sync
    def _(sy):
        P.replay("sync", sy, sems)



def _host_alibi():
    p = np.arange(128)[:, None]
    u = np.arange(1408)[None, :]
    v = (p - u + 384).astype(np.float32)
    v[p > u - 384] = -60000.0
    return v.astype(np.float16)


def _host_hb(core):
    hbv = np.full((16,), KILL if core == 0 else 0.0, dtype=np.float32)
    return np.ascontiguousarray(np.broadcast_to(hbv, (128, 16)))


def kernel(L0_attn_x, L0_attn_Wq, L0_attn_Wk, L0_attn_Wv, L0_attn_Wo,
           L0_ffn_w_gate_up, L0_ffn_w_down):
    x = np.asarray(L0_attn_x, dtype=np.float32)
    Wq = np.asarray(L0_attn_Wq, dtype=np.float32)
    Wk = np.asarray(L0_attn_Wk, dtype=np.float32)
    Wv = np.asarray(L0_attn_Wv, dtype=np.float32)
    Wo = np.asarray(L0_attn_Wo, dtype=np.float32)
    Wgu = np.asarray(L0_ffn_w_gate_up, dtype=np.float32)
    Wd = np.asarray(L0_ffn_w_down, dtype=np.float32)

    wgt = np.ascontiguousarray(Wgu.T.reshape(NCH, 128, 2, NI, 128).transpose(3, 1, 0, 2, 4))
    wdt = np.ascontiguousarray(Wd.T.reshape(NI, 128, NCH, 128).transpose(2, 1, 0, 3))
    wq = np.ascontiguousarray(Wq.reshape(NCH, 128, NHEAD, 128).transpose(2, 1, 0, 3))
    wk = np.ascontiguousarray(Wk.reshape(NCH, 128, NKV, 128).transpose(1, 0, 2, 3))
    wv = np.ascontiguousarray(Wv.reshape(NCH, 128, 512).transpose(1, 0, 2))
    wo = np.ascontiguousarray(Wo.reshape(NHEAD, 128, NCH, 128).transpose(2, 1, 0, 3))
    alibi = _host_alibi()

    nc = build(_STAGE)
    in_maps = []
    for c in range(NCORES):
        xs = np.ascontiguousarray(
            x[0, c * TPC:(c + 1) * TPC, :].T.reshape(NCH, 128, TPC).transpose(1, 0, 2))
        in_maps.append({
            "xt": xs, "wgt": wgt, "wdt": wdt, "wq": wq, "wk": wk, "wv": wv,
            "wo": wo, "alibi": alibi, "hb": _host_hb(c),
        })
    res = run_bass_kernel_spmd(nc, in_maps, core_ids=list(range(NCORES)))
    outp = np.empty((1, T, D), dtype=np.float32)
    for c in range(NCORES):
        o = np.asarray(res.results[c]["out"])
        outp[0, c * TPC:(c + 1) * TPC, :] = o.transpose(2, 1, 0).reshape(TPC, D)
    return outp


# version 13
# speedup vs baseline: 1.0442x; 1.0442x over previous
import math
import numpy as np
import ml_dtypes
import concourse.bass as bass
import concourse.mybir as mybir
from concourse.bass_utils import run_bass_kernel_spmd

F32 = mybir.dt.float32
BF16 = mybir.dt.bfloat16
FP16 = mybir.dt.float16
AF = mybir.ActivationFunctionType
ALU = mybir.AluOpType

NCORES = 8
T = 8192
D = 2048
TPC = T // NCORES
NCH = D // 128
INTER = 8192
NI = INTER // 128
NHEAD = 16
NKV = 4
HD = 128
HALO = 512
SLOPES = [2.0 ** (-0.5 * (g + 1)) for g in range(NKV)]
WINDOWS = [int(math.ceil(100.0 / s)) for s in SLOPES]
EPS = 1e-5
QSCALE = 1.0 / math.sqrt(HD)
KILL = -1e30

ENGS = ("tensor", "vector", "scalar", "gpsimd", "sync")
SEMS = ("wg0", "wg1", "wd0", "wd1", "gx", "gb", "sdma", "sdx0", "sdx1", "mm", "vv", "ss", "cc")
SLOT_SEM = ("wg0", "wg1", "wd0", "wd1")

_STAGE = "full"


def _chunks(g, qt):
    q0 = HALO + 512 * qt
    nlo = max(0, (q0 - WINDOWS[g]) // 128)
    nhi = q0 // 128 + 3
    return list(range(nlo, nhi + 1))


class Prog:

    def __init__(self):
        self.streams = {k: [] for k in ENGS}
        self.cnt = {s: 0 for s in SEMS}
        self.lastwait = {k: {} for k in ENGS}

    OWN = {"vector": "vv", "scalar": "ss"}

    def emit(self, eng, fn, waits=(), inc=None):
        wl = []
        own = self.OWN.get(eng)
        allw = list(waits)
        if own is not None:
            assert inc == (own, 1), (eng, inc)
            if self.cnt[own] > 0:
                allw.append((own, self.cnt[own]))
        for w in allw:
            if w is None:
                continue
            s, v = w
            if v and v > 0 and self.lastwait[eng].get(s, -1) < v:
                wl.append((s, v))
                self.lastwait[eng][s] = v
        val = None
        if inc is not None:
            self.cnt[inc[0]] += inc[1]
            val = self.cnt[inc[0]]
        self.streams[eng].append((tuple(wl), fn, inc))
        return val

    def replay(self, name, eng, sems):
        for wl, fn, inc in self.streams[name]:
            for s, v in wl:
                eng.wait_ge(sems[s], v)
            instr = fn(eng) if fn is not None else None
            if inc is not None:
                if instr is None:
                    instr = eng.nop()
                instr.then_inc(sems[inc[0]], inc[1])


def build(stage):
    nc = bass.Bass("TRN2", target_bir_lowering=False)

    xt = nc.dram_tensor("xt", [128, NCH, TPC], F32, kind="ExternalInput").ap()
    wgt = nc.dram_tensor("wgt", [NI, 128, NCH, 2, 128], F32, kind="ExternalInput").ap()
    wdt = nc.dram_tensor("wdt", [NCH, 128, NI, 128], F32, kind="ExternalInput").ap()
    wq = nc.dram_tensor("wq", [NHEAD, 128, NCH, 128], F32, kind="ExternalInput").ap()
    wk = nc.dram_tensor("wk", [128, NCH, NKV, 128], F32, kind="ExternalInput").ap()
    wv = nc.dram_tensor("wv", [128, NCH, 512], F32, kind="ExternalInput").ap()
    wo = nc.dram_tensor("wo", [NCH, 128, NHEAD, 128], F32, kind="ExternalInput").ap()
    alibi = nc.dram_tensor("alibi", [128, 1408], FP16, kind="ExternalInput").ap()
    hb = nc.dram_tensor("hb", [128, 16], F32, kind="ExternalInput").ap()
    out = nc.dram_tensor("out", [128, NCH, TPC], F32, kind="ExternalOutput").ap()
    cc_in = nc.dram_tensor("cc_in", [2, 512, 512], BF16).ap()
    cc_out = nc.dram_tensor("cc_out", [16, 512, 512], BF16).ap()

    from contextlib import ExitStack
    with ExitStack() as ctx:
        e = ctx.enter_context
        x_sb = e(nc.sbuf_tensor("x_sb", [128, NCH * TPC], F32))
        xn_sb = e(nc.sbuf_tensor("xn_sb", [128, NCH * TPC], BF16))
        uni = e(nc.sbuf_tensor("uni", [128, 32768], BF16))
        wgu = e(nc.sbuf_tensor("wgu", [128, 8192], BF16))
        wdb = e(nc.sbuf_tensor("wdb", [128, 8192], BF16))
        al_sb = e(nc.sbuf_tensor("al_sb", [128, 1408], FP16))
        hb_sb = e(nc.sbuf_tensor("hb_sb", [128, 16], F32))
        sgb = e(nc.sbuf_tensor("sgb", [128, 1024], BF16))
        stats = e(nc.sbuf_tensor("stats", [1, 2048], F32))
        tmp512 = e(nc.sbuf_tensor("tmp512", [128, 512], F32))
        ones_bf = e(nc.sbuf_tensor("ones_bf", [128, 1], BF16))
        ones_f32 = e(nc.sbuf_tensor("ones_f32", [1, 128], F32))

        pm = [e(nc.psum_tensor(f"pm{i}", [128, 512], F32)) for i in range(4)]
        pmA = [e(nc.psum_tensor(f"pmA{i}", [128, 512], F32)) for i in range(2)]
        plb = [e(nc.psum_tensor(f"pl{i}", [128, 512], F32)) for i in range(2)]
        PB = pm + pmA + plb

        sems = {s: e(nc.semaphore(s)) for s in SEMS}
        block = e(nc.Block())

        P = Prog()
        pst = {id(t): {"wr": None, "rd": None} for t in PB}

        def xnv(c, lo, n):
            return xn_sb[:, :].rearrange("p (c t) -> p c t", c=NCH)[:, c, lo:lo + n]

        def xv(c, lo, n):
            return x_sb[:, :].rearrange("p (c t) -> p c t", c=NCH)[:, c, lo:lo + n]

        def mm_group(ps, tiles, waits=(), rows=128, tile_waits=None):
            st = pst[id(ps)]
            w = list(waits)
            if st["rd"] is not None:
                w.append(st["rd"])
            n = len(tiles)
            val = None
            for k, (lf, rf) in enumerate(tiles):
                first, last = (k == 0), (k == n - 1)
                wk = list(w) if k == 0 else []
                if tile_waits is not None and tile_waits[k] is not None:
                    wk.append(tile_waits[k])
                def fn(eng, lf=lf, rf=rf, first=first, last=last, ps=ps, rows=rows):
                    return eng.matmul(ps[0:rows, :], lf(), rf(), start=first, stop=last)
                val = P.emit("tensor", fn, waits=wk,
                             inc=(("mm", 1) if last else None))
            st["wr"] = ("mm", val)
            return val

        def psum_read(ps, sem, val):
            pst[id(ps)]["rd"] = (sem, val)

        KOFF, VOFF, QYOFF, SPOFF = 0, NKV * 1536, 12288, 28672

        def Kap(kv, t0, n):
            return uni[:, KOFF + kv * 1536 + t0: KOFF + kv * 1536 + t0 + n]

        def Vap(n, d0, nd):
            return uni[:, VOFF + n * 512 + d0: VOFF + n * 512 + d0 + nd]

        def QYap(h, qt):
            o = QYOFF + h * 1024 + qt * 512
            return uni[:, o:o + 512]

        def Hap(tau, kk):
            o = (tau * 32 + kk) * 512
            return uni[:, o:o + 512]

        def Sap(slot):
            return uni[:, SPOFF + slot * 512: SPOFF + (slot + 1) * 512]

        def stat(i):
            return stats[:, i * 512:(i + 1) * 512]

        def slot_ap(slot, off, n):
            buf = wgu if slot < 2 else wdb
            base = (slot % 2) * 4096
            return buf[:, base + off: base + off + n]

        ones_rdy = P.emit("vector", lambda v: v.memset(ones_bf[:, :], 1.0), inc=("vv", 1))
        P.emit("vector", lambda v: v.memset(ones_f32[:, :], 1.0), inc=("vv", 1))
        for tau in range(2):
            def fn(sy, tau=tau):
                return sy.dma_start(
                    out=x_sb[:, :].rearrange("p (c t) -> p c t", c=NCH)[:, :, tau * 512:(tau + 1) * 512],
                    in_=xt[:, :, tau * 512:(tau + 1) * 512])
            P.emit("sync", fn, inc=("sdx%d" % tau, 16))
        P.emit("sync", lambda sy: sy.dma_start(out=al_sb[:, :], in_=alibi), inc=("sdma", 16))
        P.emit("sync", lambda sy: sy.dma_start(out=hb_sb[:, :], in_=hb), inc=("sdma", 16))
        LOADS = 32

        slot_cons = [None, None, None, None]
        sg_cons = [None, None]
        sp_cons = [None, None]
        last_stat_mm = [None]

        def load_wgt(i, slot):
            wsem = SLOT_SEM[slot]
            w = [slot_cons[slot]]
            def fdma(gp, i=i, slot=slot):
                return gp.dma_start(
                    out=slot_ap(slot, 0, 4096).rearrange(
                        "p (j g o) -> p j g o", j=NCH, g=2),
                    in_=wgt[i])
            return P.emit("gpsimd", fdma, waits=w, inc=(wsem, 16))

        def ln_start():
            return {"n": [0, 0], "mmv": {}, "vp": None, "sl": 0,
                    "pb": {(0, 'm'): PB[4], (0, 'q'): PB[5], (1, 'm'): PB[6], (1, 'q'): PB[7]}}

        def ln_cast(tau, c, waits):
            def fc(gp, tau=tau, c=c):
                xr = x_sb[:, :].rearrange("p (c t) -> p c t", c=NCH)
                nr = xn_sb[:, :].rearrange("p (c t) -> p c t", c=NCH)
                if c is None:
                    return gp.dma_start(out=nr[:, :, tau * 512:(tau + 1) * 512],
                                        in_=xr[:, :, tau * 512:(tau + 1) * 512])
                return gp.dma_start(out=nr[:, c, tau * 512:(tau + 1) * 512],
                                    in_=xr[:, c, tau * 512:(tau + 1) * 512])
            return P.emit("gpsimd", fc, waits=waits, inc=("gx", 16))

        def ln_chunk(L, tau, c, cast_val):
            if L["vp"] is None:
                L["vp"] = [sg_cons[0], sg_cons[1]]
            pM, pQ = L["pb"][(tau, 'm')], L["pb"][(tau, 'q')]
            k = L["n"][tau]
            first, last = k == 0, k == NCH - 1
            wm = [("gx", cast_val), ("vv", ones_rdy)]
            if first:
                wm.append(pst[id(pM)]["rd"])
            def fm(eng, tau=tau, c=c, pM=pM, first=first, last=last):
                return eng.matmul(pM[0:1, :], ones_bf[:, :], xnv(c, tau * 512, 512),
                                  start=first, stop=last)
            mM = P.emit("tensor", fm, waits=wm, inc=("mm", 1))
            sl = L["sl"]
            L["sl"] ^= 1
            sq = P.emit("scalar",
                        lambda s, c=c, sl=sl, tau=tau: s.activation(
                            sgb[:, sl * 512:(sl + 1) * 512], xnv(c, tau * 512, 512), AF.Square),
                        waits=[("gx", cast_val), L["vp"][sl]], inc=("ss", 1))
            wq = [("ss", sq)]
            if first:
                wq.append(pst[id(pQ)]["rd"])
            def fq(eng, sl=sl, pQ=pQ, first=first, last=last):
                return eng.matmul(pQ[0:1, :], ones_bf[:, :], sgb[:, sl * 512:(sl + 1) * 512],
                                  start=first, stop=last)
            mQ = P.emit("tensor", fq, waits=wq, inc=("mm", 1))
            L["vp"][sl] = ("mm", mQ)
            L["n"][tau] += 1
            if last:
                L["mmv"][tau] = (mM, mQ)
                pst[id(pM)]["wr"] = ("mm", mM)
                pst[id(pQ)]["wr"] = ("mm", mQ)

        def ln_finale(L, stat_guard=None):
            inv_d = 1.0 / D
            binfo = {}
            for tau in range(2):
                so = 0
                pM, pQ = L["pb"][(tau, 'm')], L["pb"][(tau, 'q')]
                mM, mQ = L["mmv"][tau]
                sgw = [stat_guard] if stat_guard is not None else []
                if tau == 1:
                    sgw.append(("mm", binfo[0]))
                P.emit("vector", lambda v, pM=pM, so=so: v.tensor_scalar(
                    stat(so), pM[0:1, :], inv_d, None, ALU.mult),
                       waits=[("mm", mM)] + sgw, inc=("vv", 1))
                v2 = P.emit("vector", lambda v, pQ=pQ, so=so: v.tensor_scalar(
                    stat(so + 1), pQ[0:1, :], inv_d, EPS, ALU.mult, ALU.add),
                            waits=[("mm", mQ)], inc=("vv", 1))
                P.emit("vector", lambda v, so=so: v.tensor_tensor(
                    stat(so + 2), stat(so), stat(so), ALU.mult), inc=("vv", 1))
                v4 = P.emit("vector", lambda v, so=so: v.tensor_tensor(
                    stat(so + 1), stat(so + 1), stat(so + 2), ALU.subtract), inc=("vv", 1))
                s1 = P.emit("scalar", lambda s, so=so: s.activation(
                    stat(so + 2), stat(so + 1), AF.Sqrt), waits=[("vv", v4)], inc=("ss", 1))
                P.emit("vector", lambda v, so=so: v.reciprocal(stat(so + 3), stat(so + 2)),
                       waits=[("ss", s1)], inc=("vv", 1))
                P.emit("vector", lambda v, so=so: v.tensor_tensor(
                    stat(so + 2), stat(so + 3), stat(so + 3), ALU.mult), inc=("vv", 1))
                P.emit("vector", lambda v, so=so: v.tensor_tensor(
                    stat(so + 2), stat(so + 2), stat(so + 1), ALU.mult), inc=("vv", 1))
                P.emit("vector", lambda v, so=so: v.tensor_scalar(
                    stat(so + 2), stat(so + 2), -0.5, 1.5, ALU.mult, ALU.add), inc=("vv", 1))
                v9 = P.emit("vector", lambda v, so=so: v.tensor_tensor(
                    stat(so + 3), stat(so + 3), stat(so + 2), ALU.mult), inc=("vv", 1))
                vB = P.emit("vector", lambda v, so=so: v.scalar_tensor_tensor(
                    stat(so), stat(so), -1.0, stat(so + 3), ALU.mult, ALU.mult), inc=("vv", 1))
                psum_read(pM, "vv", vB)
                psum_read(pQ, "vv", v2)
                bA = mm_group(pM, [((lambda: ones_f32[:, :]), (lambda so=so: stat(so + 3)))],
                              waits=[("vv", v9)])
                bB = mm_group(pQ, [((lambda: ones_f32[:, :]), (lambda so=so: stat(so)))],
                              waits=[("vv", vB)])
                binfo[tau] = bB
                last_stat_mm[0] = ("mm", bB)
            xn_vv = [[None] * NCH, [None] * NCH]
            for c in range(NCH):
                for tau in range(2):
                    pM, pQ = L["pb"][(tau, 'm')], L["pb"][(tau, 'q')]
                    wts = [("mm", binfo[tau])] if c == 0 else ()
                    P.emit("vector",
                           lambda v, c=c, tau=tau, pM=pM: v.tensor_tensor(
                               tmp512[:, :], xv(c, tau * 512, 512), pM[:, :], ALU.mult),
                           waits=wts, inc=("vv", 1))
                    xn_vv[tau][c] = P.emit("vector",
                                           lambda v, c=c, tau=tau, pQ=pQ: v.tensor_tensor(
                                               xnv(c, tau * 512, 512), tmp512[:, :], pQ[:, :], ALU.add),
                                           inc=("vv", 1))
            for tau in range(2):
                psum_read(L["pb"][(tau, 'm')], "vv", xn_vv[tau][-1])
                psum_read(L["pb"][(tau, 'q')], "vv", xn_vv[tau][-1])
            sg_cons[0], sg_cons[1] = L["vp"][0], L["vp"][1]
            return xn_vv

        def emit_ffn(xn_vv, preload=None, h_guard=None, store=False, ln_job=None):
            vres = [None, None]
            m_down = None
            guard = h_guard
            ln_pend = []
            for half in range(2):
                hv_last = [None, None]
                for ii in range(32):
                    i = half * 32 + ii
                    slot = i % 4 if ii < 28 else i % 2
                    if preload is not None and i in preload:
                        ld = preload.pop(i)
                    else:
                        ld = load_wgt(i, slot)
                    wsem = SLOT_SEM[slot]
                    B = PB[0:4] if ii % 2 == 0 else PB[4:8]
                    grp = {}
                    for j in range(NCH):
                        for gi in range(2):
                            for tau in range(2):
                                ps = B[gi * 2 + tau]
                                first, last = j == 0, j == NCH - 1
                                w = [("vv", xn_vv[tau][j])]
                                if first:
                                    w += [(wsem, ld), pst[id(ps)]["rd"]]
                                def fn(eng, slot=slot, j=j, gi=gi, tau=tau, ps=ps,
                                       first=first, last=last):
                                    return eng.matmul(ps[:, :], slot_ap(slot, j * 256 + gi * 128, 128),
                                                      xnv(j, tau * 512, 512), start=first, stop=last)
                                val = P.emit("tensor", fn, waits=w,
                                             inc=(("mm", 1) if last else None))
                                if last:
                                    grp[(gi, tau)] = val
                                    pst[id(ps)]["wr"] = ("mm", val)
                    slot_cons[slot] = ("mm", val)
                    for tau in range(2):
                        sl = tau
                        sU = P.emit("scalar",
                                    lambda s, sl=sl, pG=B[tau]: s.activation(
                                        sgb[:, sl * 512:(sl + 1) * 512], pG[:, :], AF.Silu),
                                    waits=[("mm", grp[(0, tau)]), sg_cons[sl]], inc=("ss", 1))
                        psum_read(B[tau], "ss", sU)
                        hw = [("ss", sU), ("mm", grp[(1, tau)])]
                        if guard is not None:
                            hw.append(guard)
                        hv = P.emit("vector",
                                    lambda v, tau=tau, ii=ii, sl=sl, pU=B[2 + tau]: v.scalar_tensor_tensor(
                                        Hap(tau, ii), pU[:, :], 0.5, sgb[:, sl * 512:(sl + 1) * 512],
                                        ALU.mult, ALU.mult),
                                    waits=hw, inc=("vv", 1))
                        psum_read(B[2 + tau], "vv", hv)
                        sg_cons[sl] = ("vv", hv)
                        hv_last[tau] = hv
                for o in range(NCH):
                    wslot = 2 + o % 2
                    wsem = SLOT_SEM[wslot]
                    def fdma(gp, o=o, half=half, wslot=wslot):
                        return gp.dma_start(
                            out=slot_ap(wslot, 0, 4096).rearrange("p (q c) -> p q c", q=32),
                            in_=wdt[o, :, half * 32:(half + 1) * 32, :])
                    ld = P.emit("gpsimd", fdma, waits=[slot_cons[wslot]], inc=(wsem, 16))
                    pd = (PB[0], PB[1]) if o % 2 == 0 else (PB[2], PB[3])
                    md = [None, None]
                    for kk in range(32):
                        for tau in range(2):
                            ps = pd[tau]
                            first, last = kk == 0, kk == 31
                            w = []
                            if first:
                                w = [(wsem, ld), ("vv", hv_last[tau]), pst[id(ps)]["rd"]]
                            def fn(eng, wslot=wslot, kk=kk, tau=tau, ps=ps, first=first, last=last):
                                return eng.matmul(ps[:, :], slot_ap(wslot, kk * 128, 128),
                                                  Hap(tau, kk), start=first, stop=last)
                            md[tau] = P.emit("tensor", fn, waits=w,
                                             inc=(("mm", 1) if last else None))
                            if last:
                                pst[id(ps)]["wr"] = ("mm", md[tau])
                    slot_cons[wslot] = ("mm", md[1])
                    m_down = md[1]
                    for tau in range(2):
                        vres[tau] = P.emit("vector",
                                           lambda v, o=o, tau=tau, ps=pd[tau]: v.tensor_tensor(
                                               xv(o, tau * 512, 512), xv(o, tau * 512, 512), ps[:, :], ALU.add),
                                           waits=[("mm", md[tau])], inc=("vv", 1))
                        psum_read(pd[tau], "vv", vres[tau])
                        if store and half == 1:
                            def fst(sy, o=o, tau=tau):
                                return sy.dma_start(
                                    out=out[:, o, tau * 512:(tau + 1) * 512],
                                    in_=xv(o, tau * 512, 512))
                            P.emit("sync", fst, waits=[("vv", vres[tau])], inc=("sdma", 16))
                    if ln_job is not None and half == 1:
                        ln_pend.append((o, vres[0], vres[1]))
                        if len(ln_pend) > 1:
                            po, pv0, pv1 = ln_pend.pop(0)
                            for tau, pv in ((0, pv0), (1, pv1)):
                                cv = ln_cast(tau, po, [("vv", pv)])
                                ln_chunk(ln_job, tau, po, cv)
                guard = ("mm", m_down)
            if ln_job is not None:
                for po, pv0, pv1 in ln_pend:
                    for tau, pv in ((0, pv0), (1, pv1)):
                        cv = ln_cast(tau, po, [("vv", pv)])
                        ln_chunk(ln_job, tau, po, cv)
            return vres, m_down

        L1 = ln_start()
        cast0 = ln_cast(0, None, [("sdx0", 16)])
        preload1 = {0: load_wgt(0, 0), 1: load_wgt(1, 1)}
        cast1 = ln_cast(1, None, [("sdx1", 16)])
        preload1[2] = load_wgt(2, 2)
        preload1[3] = load_wgt(3, 3)
        for c in range(NCH):
            ln_chunk(L1, 0, c, cast0)
            ln_chunk(L1, 1, c, cast1)
        ln1_vv = ln_finale(L1)
        L2 = ln_start()
        ffn1, m_ffn1 = emit_ffn(ln1_vv, preload=preload1, ln_job=L2)

        if stage == "ffn1":
            _emit_store(P, x_sb, out, [("vv", ffn1[0]), ("vv", ffn1[1])])
            _close(P, block, sems)
            return nc

        ldk = P.emit("gpsimd",
                     lambda gp: gp.dma_start(
                         out=wgu[:, :].rearrange("p (j kv d) -> p j kv d", j=NCH, kv=NKV), in_=wk),
                     waits=[slot_cons[0], slot_cons[1]], inc=("wg0", 16))
        ldv = P.emit("gpsimd",
                     lambda gp: gp.dma_start(
                         out=wdb[:, :].rearrange("p (j d) -> p j d", j=NCH), in_=wv),
                     waits=[slot_cons[2], slot_cons[3]], inc=("wd0", 16))

        ln2_vv = ln_finale(L2, stat_guard=last_stat_mm[0])

        k_evac = None
        m_last = None
        for kv in range(NKV):
            for qt in range(2):
                pK = pm[(kv * 2 + qt) % 2]
                tiles = [((lambda j=j, kv=kv: wgu[:, j * 512 + kv * 128: j * 512 + (kv + 1) * 128]),
                          (lambda j=j, qt=qt: xnv(j, qt * 512, 512))) for j in range(NCH)]
                tw = [("vv", ln2_vv[qt][j]) for j in range(NCH)]
                mK = mm_group(pK, tiles, waits=[("wg0", ldk)], tile_waits=tw)
                m_last = mK
                k_evac = P.emit("scalar",
                                lambda s, kv=kv, qt=qt, pK=pK: s.activation(
                                    Kap(kv, HALO + qt * 512, 512), pK[:, :], AF.Copy),
                                waits=[("mm", mK)], inc=("ss", 1))
                psum_read(pK, "ss", k_evac)
        slot_cons[0] = slot_cons[1] = ("mm", m_last)
        v_evac = None
        for w8 in range(8):
            pV = pm[w8 % 2]
            tiles = [((lambda j=j, w8=w8: xnv(j, w8 * 128, 128)),
                      (lambda j=j: wdb[:, j * 512:(j + 1) * 512])) for j in range(NCH)]
            tw = [("vv", ln2_vv[w8 // 4][j]) for j in range(NCH)]
            mV = mm_group(pV, tiles, waits=[("wd0", ldv)], tile_waits=tw)
            m_last = mV
            v_evac = P.emit("scalar",
                            lambda s, w8=w8, pV=pV: s.activation(Vap(4 + w8, 0, 512), pV[:, :], AF.Copy),
                            waits=[("mm", mV)], inc=("ss", 1))
            psum_read(pV, "ss", v_evac)
        slot_cons[2] = slot_cons[3] = ("mm", m_last)

        def f_bk(gp):
            return gp.dma_start(
                out=cc_in[0].rearrange("(kv p) t -> p kv t", p=128),
                in_=uni[:, KOFF:KOFF + NKV * 1536].rearrange(
                    "p (kv t) -> p kv t", kv=NKV)[:, :, 1024:1536])
        P.emit("gpsimd", f_bk, waits=[("ss", k_evac)], inc=("gb", 16))
        def f_bv(gp):
            return gp.dma_start(
                out=cc_in[1].rearrange("(w p) d -> p w d", p=128),
                in_=uni[:, VOFF + 8 * 512: VOFF + 12 * 512].rearrange("p (w d) -> p w d", w=4))
        bv = P.emit("gpsimd", f_bv, waits=[("ss", v_evac)], inc=("gb", 16))

        ccv = P.emit("gpsimd",
                     lambda gp: gp.collective_compute(
                         "AllGather", ALU.bypass, replica_groups=[list(range(NCORES))],
                         ins=[cc_in], outs=[cc_out]),
                     waits=[("gb", bv)], inc=("cc", 1))

        q_evac = {}
        for h in range(NHEAD):
            par = h % 2
            wsem = "wg%d" % par
            def fdma(gp, h=h, par=par):
                return gp.dma_start(
                    out=wgu[:, par * 4096: par * 4096 + 2048].rearrange("p (j e) -> p j e", j=NCH),
                    in_=wq[h])
            ldq = P.emit("gpsimd", fdma, waits=[slot_cons[par]], inc=(wsem, 16))
            mq = None
            for qt in range(2):
                pQ = pm[2 + (h * 2 + qt) % 2]
                tiles = [((lambda j=j, par=par: wgu[:, par * 4096 + j * 128: par * 4096 + (j + 1) * 128]),
                          (lambda j=j, qt=qt: xnv(j, qt * 512, 512))) for j in range(NCH)]
                mq = mm_group(pQ, tiles, waits=[(wsem, ldq)],
                              tile_waits=[("vv", ln2_vv[qt][j]) for j in range(NCH)])
                q_evac[(h, qt)] = P.emit("scalar",
                                         lambda s, h=h, qt=qt, pQ=pQ: s.mul(QYap(h, qt), pQ[:, :], QSCALE),
                                         waits=[("mm", mq)], inc=("ss", 1))
                psum_read(pQ, "ss", q_evac[(h, qt)])
            slot_cons[par] = ("mm", mq)

        def f_halo(gp):
            cid = nc.partition_id([gp.engine])
            for i in gp.Switch(cid, NCORES):
                r = (i - 1) % NCORES
                gp.dma_start(
                    out=uni[:, KOFF:KOFF + NKV * 1536].rearrange(
                        "p (kv t) -> p kv t", kv=NKV)[:, :, 0:512],
                    in_=cc_out[2 * r].rearrange("(kv p) t -> p kv t", p=128)
                ).then_inc(sems["gb"], 16)
                gp.dma_start(
                    out=uni[:, VOFF:VOFF + 4 * 512].rearrange("p (w d) -> p w d", w=4),
                    in_=cc_out[2 * r + 1].rearrange("(w p) d -> p w d", p=128)
                ).then_inc(sems["gb"], 16)
        P.cnt["gb"] += 32
        halo_done = P.cnt["gb"]
        P.emit("gpsimd", lambda gp: (f_halo(gp), None)[1], waits=[("cc", ccv)])

        SPS = [pm[0], pm[1], pmA[0]]
        sp_cons = [("mm", m_ffn1)] * 3
        binv = sgb[:, :].bitcast(F32)
        wo_ready_vv = None
        stat_rd = [None, None]
        pending = None
        iters = [(g, g * 4 + hh, qt) for g in range(NKV) for hh in range(4) for qt in range(2)]

        def emit_norm_rest(h, qt, pY, mY, vr, sc):
            pB = pmA[1]
            mB = mm_group(pB, [((lambda: ones_f32[:, :]), (lambda sc=sc: stat(sc)))],
                          waits=[vr])
            stat_rd[sc] = ("mm", mB)
            vc = P.emit("scalar", lambda s, pB=pB: s.copy(binv[:, 0:512], pB[:, :]),
                        waits=[("mm", mB), ("vv", wo_ready_vv), sg_cons[0], sg_cons[1]], inc=("ss", 1))
            psum_read(pB, "ss", vc)
            vy = P.emit("vector",
                        lambda v, h=h, qt=qt, pY=pY: v.tensor_tensor(
                            QYap(h, qt), pY[:, :], binv[:, 0:512], ALU.mult),
                        waits=[("ss", vc), ("mm", mY)], inc=("vv", 1))
            psum_read(pY, "vv", vy)
            return vy

        first_norm = True
        for it, (g, h, qt) in enumerate(iters):
            slope = SLOPES[g]
            q0 = HALO + 512 * qt
            ch = _chunks(g, qt)
            ch = [n for n in ch if n >= 4] + [n for n in ch if n < 4]
            n_t = len(ch)
            pY = pm[2 + it % 2]
            plX = plb[it % 2]
            yw0 = pst[id(pY)]["rd"]
            lw0 = pst[id(plX)]["rd"]
            sE_of = [None] * n_t

            def emit_S(ci, h=h, qt=qt, g=g, slope=slope, q0=q0, ch=ch, sE_of=sE_of):
                n = ch[ci]
                pS = SPS[ci % 3]
                slot = ci % 3
                sw = [("ss", q_evac[(h, qt)]), pst[id(pS)]["rd"]]
                if n < 4:
                    sw.append(("gb", halo_done))
                def fs(eng, g=g, n=n, h=h, qt=qt, pS=pS):
                    return eng.matmul(pS[:, :], Kap(g, n * 128, 128), QYap(h, qt),
                                      start=True, stop=True)
                mS = P.emit("tensor", fs, waits=sw, inc=("mm", 1))
                pst[id(pS)]["wr"] = ("mm", mS)
                a = n * 128 - q0
                sl_lo = 384 - a
                aw = [("mm", mS), ("sdma", LOADS)]
                def fa(v, sl_lo=sl_lo, slope=slope, pS=pS):
                    return v.scalar_tensor_tensor(pS[:, :], al_sb[:, sl_lo:sl_lo + 512],
                                                  slope, pS[:, :], ALU.mult, ALU.add)
                vA = P.emit("vector", fa, waits=aw, inc=("vv", 1))
                pst[id(pS)]["wr"] = ("vv", vA)
                ew = [("vv", vA), sp_cons[slot]]
                if n < 4:
                    bias_arg = hb_sb[:, g * 4 + n: g * 4 + n + 1]
                else:
                    bias_arg = None
                def fe(s, slot=slot, pS=pS, bias_arg=bias_arg):
                    if bias_arg is None:
                        return s.activation(Sap(slot), pS[:, :], AF.Exp)
                    return s.activation(Sap(slot), pS[:, :], AF.Exp, bias=bias_arg)
                sE = P.emit("scalar", fe, waits=ew, inc=("ss", 1))
                psum_read(pS, "ss", sE)
                sE_of[ci] = sE

            emit_S(0)
            emit_S(1)
            emit_S(2)
            if pending is not None:
                wo_ready_vv = emit_norm_rest(*pending)
                pending = None
            mY = mL = None
            for ci in range(n_t):
                n = ch[ci]
                slot = ci % 3
                first, last = ci == 0, ci == n_t - 1
                w_y = [("ss", sE_of[ci])] + ([yw0] if first and yw0 else [])
                if n < 4:
                    w_y.append(("gb", halo_done))
                def fy(eng, g=g, n=n, slot=slot, first=first, last=last, pY=pY):
                    return eng.matmul(pY[:, :], Vap(n, g * 128, 128), Sap(slot),
                                      start=first, stop=last)
                mY = P.emit("tensor", fy, waits=w_y, inc=(("mm", 1) if last else None))
                def fl(eng, slot=slot, first=first, last=last, plX=plX):
                    return eng.matmul(plX[0:1, :], ones_bf[:, :], Sap(slot),
                                      start=first, stop=last)
                mL = P.emit("tensor", fl, waits=([lw0] if first and lw0 else ()),
                            inc=("mm", 1))
                sp_cons[slot] = ("mm", mL)
                if ci + 3 < n_t:
                    emit_S(ci + 3)
            pst[id(pY)]["wr"] = ("mm", mY)
            pst[id(plX)]["wr"] = ("mm", mL)
            sc = it % 2
            vr_w = [("mm", mL), stat_rd[sc]]
            if first_norm:
                vr_w.append(last_stat_mm[0])
                first_norm = False
            P.emit("scalar", lambda s, plX=plX, sc=sc: s.activation(stat(sc), plX[0:1, :], AF.Ln),
                   waits=vr_w, inc=("ss", 1))
            vr_v = P.emit("scalar", lambda s, sc=sc: s.activation(stat(sc), stat(sc), AF.Exp, scale=-1.0),
                          inc=("ss", 1))
            vr = ("ss", vr_v)
            psum_read(plX, "ss", vr_v)
            pending = (h, qt, pY, mY, vr, sc)
        wo_ready_vv = emit_norm_rest(*pending)
        pending = None

        sg_cons[0] = ("vv", wo_ready_vv)
        sg_cons[1] = ("vv", wo_ready_vv)
        L3 = ln_start()
        ln3_pend = []
        wo_done_vv = None
        m_wo = None
        for o in range(NCH):
            par = o % 2
            wsem = "wg%d" % par
            def fdma(gp, o=o, par=par):
                return gp.dma_start(
                    out=wgu[:, par * 4096: par * 4096 + 2048].rearrange("p (hh oo) -> p hh oo", hh=NHEAD),
                    in_=wo[o])
            ldo = P.emit("gpsimd", fdma, waits=[slot_cons[par]], inc=(wsem, 16))
            wov = [None, None]
            for qt in range(2):
                pO = pm[(o * 2 + qt) % 2]
                tiles = [((lambda hh=hh, par=par: wgu[:, par * 4096 + hh * 128: par * 4096 + (hh + 1) * 128]),
                          (lambda hh=hh, qt=qt: QYap(hh, qt))) for hh in range(NHEAD)]
                mO = mm_group(pO, tiles, waits=[(wsem, ldo), ("vv", wo_ready_vv)])
                m_wo = mO
                wo_done_vv = P.emit("vector",
                                    lambda v, o=o, qt=qt, pO=pO: v.tensor_tensor(
                                        xv(o, qt * 512, 512), xv(o, qt * 512, 512), pO[:, :], ALU.add),
                                    waits=[("mm", mO)], inc=("vv", 1))
                psum_read(pO, "vv", wo_done_vv)
                wov[qt] = wo_done_vv
            slot_cons[par] = ("mm", m_wo)
            ln3_pend.append((o, wov[0], wov[1]))
            if len(ln3_pend) > 1:
                po, pv0, pv1 = ln3_pend.pop(0)
                for tau, pv in ((0, pv0), (1, pv1)):
                    cv = ln_cast(tau, po, [("vv", pv)])
                    ln_chunk(L3, tau, po, cv)
        for po, pv0, pv1 in ln3_pend:
            for tau, pv in ((0, pv0), (1, pv1)):
                cv = ln_cast(tau, po, [("vv", pv)])
                ln_chunk(L3, tau, po, cv)

        if stage == "attn":
            _emit_store(P, x_sb, out, [("vv", wo_done_vv)])
            _close(P, block, sems)
            return nc

        preload2 = {i: load_wgt(i, i) for i in range(4)}

        ln3_vv = ln_finale(L3, stat_guard=("mm", m_wo))
        ffn2, m_ffn2 = emit_ffn(ln3_vv, preload=preload2,
                                h_guard=("mm", m_wo), store=True)
        P.emit("sync", None, waits=[("sdma", P.cnt["sdma"])])
        _close(P, block, sems)
    return nc


def _emit_store(P, x_sb, out, waits):
    def fn(sy):
        return sy.dma_start(out=out, in_=x_sb[:, :].rearrange("p (c t) -> p c t", c=NCH))
    v = P.emit("sync", fn, waits=waits, inc=("sdma", 16))
    P.emit("sync", None, waits=[("sdma", v)])


def _close(P, block, sems):
    @block.tensor
    def _(t):
        P.replay("tensor", t, sems)

    @block.vector
    def _(v):
        P.replay("vector", v, sems)

    @block.scalar
    def _(s):
        P.replay("scalar", s, sems)

    @block.gpsimd
    def _(gp):
        P.replay("gpsimd", gp, sems)

    @block.sync
    def _(sy):
        P.replay("sync", sy, sems)



def _host_alibi():
    p = np.arange(128)[:, None]
    u = np.arange(1408)[None, :]
    v = (p - u + 384).astype(np.float32)
    v[p > u - 384] = -60000.0
    return v.astype(np.float16)


def _host_hb(core):
    hbv = np.full((16,), KILL if core == 0 else 0.0, dtype=np.float32)
    return np.ascontiguousarray(np.broadcast_to(hbv, (128, 16)))


def kernel(L0_attn_x, L0_attn_Wq, L0_attn_Wk, L0_attn_Wv, L0_attn_Wo,
           L0_ffn_w_gate_up, L0_ffn_w_down):
    x = np.asarray(L0_attn_x, dtype=np.float32)
    Wq = np.asarray(L0_attn_Wq, dtype=np.float32)
    Wk = np.asarray(L0_attn_Wk, dtype=np.float32)
    Wv = np.asarray(L0_attn_Wv, dtype=np.float32)
    Wo = np.asarray(L0_attn_Wo, dtype=np.float32)
    Wgu = np.asarray(L0_ffn_w_gate_up, dtype=np.float32)
    Wd = np.asarray(L0_ffn_w_down, dtype=np.float32)

    wgt = np.ascontiguousarray(Wgu.T.reshape(NCH, 128, 2, NI, 128).transpose(3, 1, 0, 2, 4))
    wdt = np.ascontiguousarray(Wd.T.reshape(NI, 128, NCH, 128).transpose(2, 1, 0, 3))
    wq = np.ascontiguousarray(Wq.reshape(NCH, 128, NHEAD, 128).transpose(2, 1, 0, 3))
    wk = np.ascontiguousarray(Wk.reshape(NCH, 128, NKV, 128).transpose(1, 0, 2, 3))
    wv = np.ascontiguousarray(Wv.reshape(NCH, 128, 512).transpose(1, 0, 2))
    wo = np.ascontiguousarray(Wo.reshape(NHEAD, 128, NCH, 128).transpose(2, 1, 0, 3))
    alibi = _host_alibi()

    nc = build(_STAGE)
    in_maps = []
    for c in range(NCORES):
        xs = np.ascontiguousarray(
            x[0, c * TPC:(c + 1) * TPC, :].T.reshape(NCH, 128, TPC).transpose(1, 0, 2))
        in_maps.append({
            "xt": xs, "wgt": wgt, "wdt": wdt, "wq": wq, "wk": wk, "wv": wv,
            "wo": wo, "alibi": alibi, "hb": _host_hb(c),
        })
    res = run_bass_kernel_spmd(nc, in_maps, core_ids=list(range(NCORES)))
    outp = np.empty((1, T, D), dtype=np.float32)
    for c in range(NCORES):
        o = np.asarray(res.results[c]["out"])
        outp[0, c * TPC:(c + 1) * TPC, :] = o.transpose(2, 1, 0).reshape(TPC, D)
    return outp


# version 14 (final)
# speedup vs baseline: 1.0476x; 1.0032x over previous
import math
import numpy as np
import ml_dtypes
import concourse.bass as bass
import concourse.mybir as mybir
from concourse.bass_utils import run_bass_kernel_spmd

F32 = mybir.dt.float32
BF16 = mybir.dt.bfloat16
FP16 = mybir.dt.float16
AF = mybir.ActivationFunctionType
ALU = mybir.AluOpType

NCORES = 8
T = 8192
D = 2048
TPC = T // NCORES
NCH = D // 128
INTER = 8192
NI = INTER // 128
NHEAD = 16
NKV = 4
HD = 128
HALO = 512
SLOPES = [2.0 ** (-0.5 * (g + 1)) for g in range(NKV)]
WINDOWS = [int(math.ceil(100.0 / s)) for s in SLOPES]
EPS = 1e-5
QSCALE = 1.0 / math.sqrt(HD)
KILL = -1e30

ENGS = ("tensor", "vector", "scalar", "gpsimd", "sync")
SEMS = ("wg0", "wg1", "wd0", "wd1", "gx", "gy", "gb", "sdma", "sdx0", "sdx1", "mm", "vv", "ss", "cc")
SLOT_SEM = ("wg0", "wg1", "wd0", "wd1")

_STAGE = "full"


def _chunks(g, qt):
    q0 = HALO + 512 * qt
    nlo = max(0, (q0 - WINDOWS[g]) // 128)
    nhi = q0 // 128 + 3
    return list(range(nlo, nhi + 1))


class Prog:

    def __init__(self):
        self.streams = {k: [] for k in ENGS}
        self.cnt = {s: 0 for s in SEMS}
        self.lastwait = {k: {} for k in ENGS}

    OWN = {"vector": "vv", "scalar": "ss"}

    def emit(self, eng, fn, waits=(), inc=None):
        wl = []
        own = self.OWN.get(eng)
        allw = list(waits)
        if own is not None:
            assert inc == (own, 1), (eng, inc)
            if self.cnt[own] > 0:
                allw.append((own, self.cnt[own]))
        for w in allw:
            if w is None:
                continue
            s, v = w
            if v and v > 0 and self.lastwait[eng].get(s, -1) < v:
                wl.append((s, v))
                self.lastwait[eng][s] = v
        val = None
        if inc is not None:
            self.cnt[inc[0]] += inc[1]
            val = self.cnt[inc[0]]
        self.streams[eng].append((tuple(wl), fn, inc))
        return val

    def replay(self, name, eng, sems):
        for wl, fn, inc in self.streams[name]:
            for s, v in wl:
                eng.wait_ge(sems[s], v)
            instr = fn(eng) if fn is not None else None
            if inc is not None:
                if instr is None:
                    instr = eng.nop()
                instr.then_inc(sems[inc[0]], inc[1])


def build(stage):
    nc = bass.Bass("TRN2", target_bir_lowering=False)

    xt = nc.dram_tensor("xt", [128, NCH, TPC], F32, kind="ExternalInput").ap()
    wgt = nc.dram_tensor("wgt", [NI, 128, NCH, 2, 128], F32, kind="ExternalInput").ap()
    wdt = nc.dram_tensor("wdt", [NCH, 128, NI, 128], F32, kind="ExternalInput").ap()
    wq = nc.dram_tensor("wq", [NHEAD, 128, NCH, 128], F32, kind="ExternalInput").ap()
    wk = nc.dram_tensor("wk", [128, NCH, NKV, 128], F32, kind="ExternalInput").ap()
    wv = nc.dram_tensor("wv", [128, NCH, 512], F32, kind="ExternalInput").ap()
    wo = nc.dram_tensor("wo", [NCH, 128, NHEAD, 128], F32, kind="ExternalInput").ap()
    alibi = nc.dram_tensor("alibi", [128, 1408], FP16, kind="ExternalInput").ap()
    hb = nc.dram_tensor("hb", [128, 16], F32, kind="ExternalInput").ap()
    out = nc.dram_tensor("out", [128, NCH, TPC], F32, kind="ExternalOutput").ap()
    cc_in = nc.dram_tensor("cc_in", [2, 512, 512], BF16).ap()
    cc_out = nc.dram_tensor("cc_out", [16, 512, 512], BF16).ap()

    from contextlib import ExitStack
    with ExitStack() as ctx:
        e = ctx.enter_context
        x_sb = e(nc.sbuf_tensor("x_sb", [128, NCH * TPC], F32))
        xn_sb = e(nc.sbuf_tensor("xn_sb", [128, NCH * TPC], BF16))
        uni = e(nc.sbuf_tensor("uni", [128, 32768], BF16))
        wgu = e(nc.sbuf_tensor("wgu", [128, 8192], BF16))
        wdb = e(nc.sbuf_tensor("wdb", [128, 8192], BF16))
        al_sb = e(nc.sbuf_tensor("al_sb", [128, 1408], FP16))
        hb_sb = e(nc.sbuf_tensor("hb_sb", [128, 16], F32))
        sgb = e(nc.sbuf_tensor("sgb", [128, 1024], BF16))
        stats = e(nc.sbuf_tensor("stats", [1, 2048], F32))
        tmp512 = e(nc.sbuf_tensor("tmp512", [128, 512], F32))
        ones_bf = e(nc.sbuf_tensor("ones_bf", [128, 1], BF16))
        ones_f32 = e(nc.sbuf_tensor("ones_f32", [1, 128], F32))

        pm = [e(nc.psum_tensor(f"pm{i}", [128, 512], F32)) for i in range(4)]
        pmA = [e(nc.psum_tensor(f"pmA{i}", [128, 512], F32)) for i in range(2)]
        plb = [e(nc.psum_tensor(f"pl{i}", [128, 512], F32)) for i in range(2)]
        PB = pm + pmA + plb

        sems = {s: e(nc.semaphore(s)) for s in SEMS}
        block = e(nc.Block())

        P = Prog()
        pst = {id(t): {"wr": None, "rd": None} for t in PB}

        def xnv(c, lo, n):
            return xn_sb[:, :].rearrange("p (c t) -> p c t", c=NCH)[:, c, lo:lo + n]

        def xv(c, lo, n):
            return x_sb[:, :].rearrange("p (c t) -> p c t", c=NCH)[:, c, lo:lo + n]

        def mm_group(ps, tiles, waits=(), rows=128, tile_waits=None):
            st = pst[id(ps)]
            w = list(waits)
            if st["rd"] is not None:
                w.append(st["rd"])
            n = len(tiles)
            val = None
            for k, (lf, rf) in enumerate(tiles):
                first, last = (k == 0), (k == n - 1)
                wk = list(w) if k == 0 else []
                if tile_waits is not None and tile_waits[k] is not None:
                    wk.append(tile_waits[k])
                def fn(eng, lf=lf, rf=rf, first=first, last=last, ps=ps, rows=rows):
                    return eng.matmul(ps[0:rows, :], lf(), rf(), start=first, stop=last)
                val = P.emit("tensor", fn, waits=wk,
                             inc=(("mm", 1) if last else None))
            st["wr"] = ("mm", val)
            return val

        def psum_read(ps, sem, val):
            pst[id(ps)]["rd"] = (sem, val)

        KOFF, VOFF, QYOFF, SPOFF = 0, NKV * 1536, 12288, 28672

        def Kap(kv, t0, n):
            return uni[:, KOFF + kv * 1536 + t0: KOFF + kv * 1536 + t0 + n]

        def Vap(n, d0, nd):
            return uni[:, VOFF + n * 512 + d0: VOFF + n * 512 + d0 + nd]

        def QYap(h, qt):
            o = QYOFF + h * 1024 + qt * 512
            return uni[:, o:o + 512]

        def Hap(tau, kk):
            o = (tau * 32 + kk) * 512
            return uni[:, o:o + 512]

        def Sap(slot):
            return uni[:, SPOFF + slot * 512: SPOFF + (slot + 1) * 512]

        def stat(i):
            return stats[:, i * 512:(i + 1) * 512]

        def slot_ap(slot, off, n):
            buf = wgu if slot < 2 else wdb
            base = (slot % 2) * 4096
            return buf[:, base + off: base + off + n]

        ones_rdy = P.emit("vector", lambda v: v.memset(ones_bf[:, :], 1.0), inc=("vv", 1))
        P.emit("vector", lambda v: v.memset(ones_f32[:, :], 1.0), inc=("vv", 1))
        for tau in range(2):
            def fn(sy, tau=tau):
                return sy.dma_start(
                    out=x_sb[:, :].rearrange("p (c t) -> p c t", c=NCH)[:, :, tau * 512:(tau + 1) * 512],
                    in_=xt[:, :, tau * 512:(tau + 1) * 512])
            P.emit("sync", fn, inc=("sdx%d" % tau, 16))
        P.emit("sync", lambda sy: sy.dma_start(out=al_sb[:, :], in_=alibi), inc=("sdma", 16))
        P.emit("sync", lambda sy: sy.dma_start(out=hb_sb[:, :], in_=hb), inc=("sdma", 16))
        LOADS = 32

        slot_cons = [None, None, None, None]
        sg_cons = [None, None]
        sp_cons = [None, None]
        last_stat_mm = [None]

        def load_wgt(i, slot):
            wsem = SLOT_SEM[slot]
            w = [slot_cons[slot]]
            def fdma(gp, i=i, slot=slot):
                return gp.dma_start(
                    out=slot_ap(slot, 0, 4096).rearrange(
                        "p (j g o) -> p j g o", j=NCH, g=2),
                    in_=wgt[i])
            return P.emit("gpsimd", fdma, waits=w, inc=(wsem, 16))

        def ln_start():
            return {"n": [0, 0], "mmv": {}, "vp": None, "sl": 0,
                    "pb": {(0, 'm'): PB[4], (0, 'q'): PB[5], (1, 'm'): PB[6], (1, 'q'): PB[7]}}

        def ln_cast(tau, c, waits):
            sem = "gx" if tau == 0 else "gy"
            w = list(waits)
            if P.cnt[sem] > 0:
                w.append((sem, P.cnt[sem]))
            def fc(gp, tau=tau, c=c):
                xr = x_sb[:, :].rearrange("p (c t) -> p c t", c=NCH)
                nr = xn_sb[:, :].rearrange("p (c t) -> p c t", c=NCH)
                if c is None:
                    return gp.dma_start(out=nr[:, :, tau * 512:(tau + 1) * 512],
                                        in_=xr[:, :, tau * 512:(tau + 1) * 512])
                return gp.dma_start(out=nr[:, c, tau * 512:(tau + 1) * 512],
                                    in_=xr[:, c, tau * 512:(tau + 1) * 512])
            return (sem, P.emit("gpsimd", fc, waits=w, inc=(sem, 16)))

        def ln_chunk(L, tau, c, cast_val):
            if L["vp"] is None:
                L["vp"] = [sg_cons[0], sg_cons[1]]
            pM, pQ = L["pb"][(tau, 'm')], L["pb"][(tau, 'q')]
            k = L["n"][tau]
            first, last = k == 0, k == NCH - 1
            wm = [cast_val, ("vv", ones_rdy)]
            if first:
                wm.append(pst[id(pM)]["rd"])
            def fm(eng, tau=tau, c=c, pM=pM, first=first, last=last):
                return eng.matmul(pM[0:1, :], ones_bf[:, :], xnv(c, tau * 512, 512),
                                  start=first, stop=last)
            mM = P.emit("tensor", fm, waits=wm, inc=("mm", 1))
            sl = L["sl"]
            L["sl"] ^= 1
            sq = P.emit("scalar",
                        lambda s, c=c, sl=sl, tau=tau: s.activation(
                            sgb[:, sl * 512:(sl + 1) * 512], xnv(c, tau * 512, 512), AF.Square),
                        waits=[cast_val, L["vp"][sl]], inc=("ss", 1))
            wq = [("ss", sq)]
            if first:
                wq.append(pst[id(pQ)]["rd"])
            def fq(eng, sl=sl, pQ=pQ, first=first, last=last):
                return eng.matmul(pQ[0:1, :], ones_bf[:, :], sgb[:, sl * 512:(sl + 1) * 512],
                                  start=first, stop=last)
            mQ = P.emit("tensor", fq, waits=wq, inc=("mm", 1))
            L["vp"][sl] = ("mm", mQ)
            L["n"][tau] += 1
            if last:
                L["mmv"][tau] = (mM, mQ)
                pst[id(pM)]["wr"] = ("mm", mM)
                pst[id(pQ)]["wr"] = ("mm", mQ)

        def ln_finale(L, stat_guard=None):
            inv_d = 1.0 / D
            binfo = {}
            for tau in range(2):
                so = 0
                pM, pQ = L["pb"][(tau, 'm')], L["pb"][(tau, 'q')]
                mM, mQ = L["mmv"][tau]
                sgw = [stat_guard] if stat_guard is not None else []
                if tau == 1:
                    sgw.append(("mm", binfo[0]))
                P.emit("vector", lambda v, pM=pM, so=so: v.tensor_scalar(
                    stat(so), pM[0:1, :], inv_d, None, ALU.mult),
                       waits=[("mm", mM)] + sgw, inc=("vv", 1))
                v2 = P.emit("vector", lambda v, pQ=pQ, so=so: v.tensor_scalar(
                    stat(so + 1), pQ[0:1, :], inv_d, EPS, ALU.mult, ALU.add),
                            waits=[("mm", mQ)], inc=("vv", 1))
                P.emit("vector", lambda v, so=so: v.tensor_tensor(
                    stat(so + 2), stat(so), stat(so), ALU.mult), inc=("vv", 1))
                v4 = P.emit("vector", lambda v, so=so: v.tensor_tensor(
                    stat(so + 1), stat(so + 1), stat(so + 2), ALU.subtract), inc=("vv", 1))
                s1 = P.emit("scalar", lambda s, so=so: s.activation(
                    stat(so + 2), stat(so + 1), AF.Sqrt), waits=[("vv", v4)], inc=("ss", 1))
                P.emit("vector", lambda v, so=so: v.reciprocal(stat(so + 3), stat(so + 2)),
                       waits=[("ss", s1)], inc=("vv", 1))
                P.emit("vector", lambda v, so=so: v.tensor_tensor(
                    stat(so + 2), stat(so + 3), stat(so + 3), ALU.mult), inc=("vv", 1))
                P.emit("vector", lambda v, so=so: v.tensor_tensor(
                    stat(so + 2), stat(so + 2), stat(so + 1), ALU.mult), inc=("vv", 1))
                P.emit("vector", lambda v, so=so: v.tensor_scalar(
                    stat(so + 2), stat(so + 2), -0.5, 1.5, ALU.mult, ALU.add), inc=("vv", 1))
                v9 = P.emit("vector", lambda v, so=so: v.tensor_tensor(
                    stat(so + 3), stat(so + 3), stat(so + 2), ALU.mult), inc=("vv", 1))
                vB = P.emit("vector", lambda v, so=so: v.scalar_tensor_tensor(
                    stat(so), stat(so), -1.0, stat(so + 3), ALU.mult, ALU.mult), inc=("vv", 1))
                psum_read(pM, "vv", vB)
                psum_read(pQ, "vv", v2)
                bA = mm_group(pM, [((lambda: ones_f32[:, :]), (lambda so=so: stat(so + 3)))],
                              waits=[("vv", v9)])
                bB = mm_group(pQ, [((lambda: ones_f32[:, :]), (lambda so=so: stat(so)))],
                              waits=[("vv", vB)])
                binfo[tau] = bB
                last_stat_mm[0] = ("mm", bB)
            xn_vv = [[None] * NCH, [None] * NCH]
            for c in range(NCH):
                for tau in range(2):
                    pM, pQ = L["pb"][(tau, 'm')], L["pb"][(tau, 'q')]
                    wts = [("mm", binfo[tau])] if c == 0 else ()
                    P.emit("vector",
                           lambda v, c=c, tau=tau, pM=pM: v.tensor_tensor(
                               tmp512[:, :], xv(c, tau * 512, 512), pM[:, :], ALU.mult),
                           waits=wts, inc=("vv", 1))
                    xn_vv[tau][c] = P.emit("vector",
                                           lambda v, c=c, tau=tau, pQ=pQ: v.tensor_tensor(
                                               xnv(c, tau * 512, 512), tmp512[:, :], pQ[:, :], ALU.add),
                                           inc=("vv", 1))
            for tau in range(2):
                psum_read(L["pb"][(tau, 'm')], "vv", xn_vv[tau][-1])
                psum_read(L["pb"][(tau, 'q')], "vv", xn_vv[tau][-1])
            sg_cons[0], sg_cons[1] = L["vp"][0], L["vp"][1]
            return xn_vv

        def emit_ffn(xn_vv, preload=None, h_guard=None, store=False, ln_job=None):
            vres = [None, None]
            m_down = None
            guard = h_guard
            ln_pend = []
            for half in range(2):
                hv_last = [None, None]
                for ii in range(32):
                    i = half * 32 + ii
                    slot = i % 4 if ii < 28 else i % 2
                    if preload is not None and i in preload:
                        ld = preload.pop(i)
                    else:
                        ld = load_wgt(i, slot)
                    wsem = SLOT_SEM[slot]
                    B = PB[0:4] if ii % 2 == 0 else PB[4:8]
                    grp = {}
                    for j in range(NCH):
                        for gi in range(2):
                            for tau in range(2):
                                ps = B[gi * 2 + tau]
                                first, last = j == 0, j == NCH - 1
                                w = [("vv", xn_vv[tau][j])]
                                if first:
                                    w += [(wsem, ld), pst[id(ps)]["rd"]]
                                def fn(eng, slot=slot, j=j, gi=gi, tau=tau, ps=ps,
                                       first=first, last=last):
                                    return eng.matmul(ps[:, :], slot_ap(slot, j * 256 + gi * 128, 128),
                                                      xnv(j, tau * 512, 512), start=first, stop=last)
                                val = P.emit("tensor", fn, waits=w,
                                             inc=(("mm", 1) if last else None))
                                if last:
                                    grp[(gi, tau)] = val
                                    pst[id(ps)]["wr"] = ("mm", val)
                    slot_cons[slot] = ("mm", val)
                    for tau in range(2):
                        sl = tau
                        sU = P.emit("scalar",
                                    lambda s, sl=sl, pG=B[tau]: s.activation(
                                        sgb[:, sl * 512:(sl + 1) * 512], pG[:, :], AF.Silu),
                                    waits=[("mm", grp[(0, tau)]), sg_cons[sl]], inc=("ss", 1))
                        psum_read(B[tau], "ss", sU)
                        hw = [("ss", sU), ("mm", grp[(1, tau)])]
                        if guard is not None:
                            hw.append(guard)
                        hv = P.emit("vector",
                                    lambda v, tau=tau, ii=ii, sl=sl, pU=B[2 + tau]: v.scalar_tensor_tensor(
                                        Hap(tau, ii), pU[:, :], 0.5, sgb[:, sl * 512:(sl + 1) * 512],
                                        ALU.mult, ALU.mult),
                                    waits=hw, inc=("vv", 1))
                        psum_read(B[2 + tau], "vv", hv)
                        sg_cons[sl] = ("vv", hv)
                        hv_last[tau] = hv
                for o in range(NCH):
                    wslot = 2 + o % 2
                    wsem = SLOT_SEM[wslot]
                    def fdma(gp, o=o, half=half, wslot=wslot):
                        return gp.dma_start(
                            out=slot_ap(wslot, 0, 4096).rearrange("p (q c) -> p q c", q=32),
                            in_=wdt[o, :, half * 32:(half + 1) * 32, :])
                    ld = P.emit("gpsimd", fdma, waits=[slot_cons[wslot]], inc=(wsem, 16))
                    pd = (PB[0], PB[1]) if o % 2 == 0 else (PB[2], PB[3])
                    md = [None, None]
                    for kk in range(32):
                        for tau in range(2):
                            ps = pd[tau]
                            first, last = kk == 0, kk == 31
                            w = []
                            if first:
                                w = [(wsem, ld), ("vv", hv_last[tau]), pst[id(ps)]["rd"]]
                            def fn(eng, wslot=wslot, kk=kk, tau=tau, ps=ps, first=first, last=last):
                                return eng.matmul(ps[:, :], slot_ap(wslot, kk * 128, 128),
                                                  Hap(tau, kk), start=first, stop=last)
                            md[tau] = P.emit("tensor", fn, waits=w,
                                             inc=(("mm", 1) if last else None))
                            if last:
                                pst[id(ps)]["wr"] = ("mm", md[tau])
                    slot_cons[wslot] = ("mm", md[1])
                    m_down = md[1]
                    for tau in range(2):
                        vres[tau] = P.emit("vector",
                                           lambda v, o=o, tau=tau, ps=pd[tau]: v.tensor_tensor(
                                               xv(o, tau * 512, 512), xv(o, tau * 512, 512), ps[:, :], ALU.add),
                                           waits=[("mm", md[tau])], inc=("vv", 1))
                        psum_read(pd[tau], "vv", vres[tau])
                        if store and half == 1:
                            def fst(sy, o=o, tau=tau):
                                return sy.dma_start(
                                    out=out[:, o, tau * 512:(tau + 1) * 512],
                                    in_=xv(o, tau * 512, 512))
                            P.emit("sync", fst, waits=[("vv", vres[tau])], inc=("sdma", 16))
                    if ln_job is not None and half == 1:
                        ln_pend.append((o, vres[0], vres[1]))
                        if len(ln_pend) > 1:
                            po, pv0, pv1 = ln_pend.pop(0)
                            for tau, pv in ((0, pv0), (1, pv1)):
                                cv = ln_cast(tau, po, [("vv", pv)])
                                ln_chunk(ln_job, tau, po, cv)
                guard = ("mm", m_down)
            if ln_job is not None:
                for po, pv0, pv1 in ln_pend:
                    for tau, pv in ((0, pv0), (1, pv1)):
                        cv = ln_cast(tau, po, [("vv", pv)])
                        ln_chunk(ln_job, tau, po, cv)
            return vres, m_down

        L1 = ln_start()
        cast0 = ln_cast(0, None, [("sdx0", 16)])
        preload1 = {0: load_wgt(0, 0), 1: load_wgt(1, 1)}
        cast1 = ln_cast(1, None, [("sdx1", 16)])
        preload1[2] = load_wgt(2, 2)
        preload1[3] = load_wgt(3, 3)
        for c in range(NCH):
            ln_chunk(L1, 0, c, cast0)
            ln_chunk(L1, 1, c, cast1)
        ln1_vv = ln_finale(L1)
        L2 = ln_start()
        ffn1, m_ffn1 = emit_ffn(ln1_vv, preload=preload1, ln_job=L2)

        if stage == "ffn1":
            _emit_store(P, x_sb, out, [("vv", ffn1[0]), ("vv", ffn1[1])])
            _close(P, block, sems)
            return nc

        ldk = P.emit("gpsimd",
                     lambda gp: gp.dma_start(
                         out=wgu[:, :].rearrange("p (j kv d) -> p j kv d", j=NCH, kv=NKV), in_=wk),
                     waits=[slot_cons[0], slot_cons[1]], inc=("wg0", 16))
        ldv = P.emit("gpsimd",
                     lambda gp: gp.dma_start(
                         out=wdb[:, :].rearrange("p (j d) -> p j d", j=NCH), in_=wv),
                     waits=[slot_cons[2], slot_cons[3]], inc=("wd0", 16))

        ln2_vv = ln_finale(L2, stat_guard=last_stat_mm[0])

        k_evac = None
        m_last = None
        for kv in range(NKV):
            for qt in range(2):
                pK = pm[(kv * 2 + qt) % 2]
                tiles = [((lambda j=j, kv=kv: wgu[:, j * 512 + kv * 128: j * 512 + (kv + 1) * 128]),
                          (lambda j=j, qt=qt: xnv(j, qt * 512, 512))) for j in range(NCH)]
                tw = [("vv", ln2_vv[qt][j]) for j in range(NCH)]
                mK = mm_group(pK, tiles, waits=[("wg0", ldk)], tile_waits=tw)
                m_last = mK
                k_evac = P.emit("scalar",
                                lambda s, kv=kv, qt=qt, pK=pK: s.activation(
                                    Kap(kv, HALO + qt * 512, 512), pK[:, :], AF.Copy),
                                waits=[("mm", mK)], inc=("ss", 1))
                psum_read(pK, "ss", k_evac)
        slot_cons[0] = slot_cons[1] = ("mm", m_last)
        v_evac = None
        for w8 in range(8):
            pV = pm[w8 % 2]
            tiles = [((lambda j=j, w8=w8: xnv(j, w8 * 128, 128)),
                      (lambda j=j: wdb[:, j * 512:(j + 1) * 512])) for j in range(NCH)]
            tw = [("vv", ln2_vv[w8 // 4][j]) for j in range(NCH)]
            mV = mm_group(pV, tiles, waits=[("wd0", ldv)], tile_waits=tw)
            m_last = mV
            v_evac = P.emit("scalar",
                            lambda s, w8=w8, pV=pV: s.activation(Vap(4 + w8, 0, 512), pV[:, :], AF.Copy),
                            waits=[("mm", mV)], inc=("ss", 1))
            psum_read(pV, "ss", v_evac)
        slot_cons[2] = slot_cons[3] = ("mm", m_last)

        def f_bk(gp):
            return gp.dma_start(
                out=cc_in[0].rearrange("(kv p) t -> p kv t", p=128),
                in_=uni[:, KOFF:KOFF + NKV * 1536].rearrange(
                    "p (kv t) -> p kv t", kv=NKV)[:, :, 1024:1536])
        P.emit("gpsimd", f_bk, waits=[("ss", k_evac)], inc=("gb", 16))
        def f_bv(gp):
            return gp.dma_start(
                out=cc_in[1].rearrange("(w p) d -> p w d", p=128),
                in_=uni[:, VOFF + 8 * 512: VOFF + 12 * 512].rearrange("p (w d) -> p w d", w=4))
        bv = P.emit("gpsimd", f_bv, waits=[("ss", v_evac)], inc=("gb", 16))

        ccv = P.emit("gpsimd",
                     lambda gp: gp.collective_compute(
                         "AllGather", ALU.bypass, replica_groups=[list(range(NCORES))],
                         ins=[cc_in], outs=[cc_out]),
                     waits=[("gb", bv)], inc=("cc", 1))

        q_evac = {}
        for h in range(NHEAD):
            par = h % 2
            wsem = "wg%d" % par
            def fdma(gp, h=h, par=par):
                return gp.dma_start(
                    out=wgu[:, par * 4096: par * 4096 + 2048].rearrange("p (j e) -> p j e", j=NCH),
                    in_=wq[h])
            ldq = P.emit("gpsimd", fdma, waits=[slot_cons[par]], inc=(wsem, 16))
            mq = None
            for qt in range(2):
                pQ = pm[2 + (h * 2 + qt) % 2]
                tiles = [((lambda j=j, par=par: wgu[:, par * 4096 + j * 128: par * 4096 + (j + 1) * 128]),
                          (lambda j=j, qt=qt: xnv(j, qt * 512, 512))) for j in range(NCH)]
                mq = mm_group(pQ, tiles, waits=[(wsem, ldq)],
                              tile_waits=[("vv", ln2_vv[qt][j]) for j in range(NCH)])
                q_evac[(h, qt)] = P.emit("scalar",
                                         lambda s, h=h, qt=qt, pQ=pQ: s.mul(QYap(h, qt), pQ[:, :], QSCALE),
                                         waits=[("mm", mq)], inc=("ss", 1))
                psum_read(pQ, "ss", q_evac[(h, qt)])
            slot_cons[par] = ("mm", mq)

        def f_halo(gp):
            cid = nc.partition_id([gp.engine])
            for i in gp.Switch(cid, NCORES):
                r = (i - 1) % NCORES
                gp.dma_start(
                    out=uni[:, KOFF:KOFF + NKV * 1536].rearrange(
                        "p (kv t) -> p kv t", kv=NKV)[:, :, 0:512],
                    in_=cc_out[2 * r].rearrange("(kv p) t -> p kv t", p=128)
                ).then_inc(sems["gb"], 16)
                gp.dma_start(
                    out=uni[:, VOFF:VOFF + 4 * 512].rearrange("p (w d) -> p w d", w=4),
                    in_=cc_out[2 * r + 1].rearrange("(w p) d -> p w d", p=128)
                ).then_inc(sems["gb"], 16)
        P.cnt["gb"] += 32
        halo_done = P.cnt["gb"]
        P.emit("gpsimd", lambda gp: (f_halo(gp), None)[1], waits=[("cc", ccv)])

        SPS = [pm[0], pm[1], pmA[0]]
        sp_cons = [("mm", m_ffn1)] * 3
        binv = sgb[:, :].bitcast(F32)
        wo_ready_vv = None
        stat_rd = [None, None]
        pending = None
        iters = [(g, g * 4 + hh, qt) for g in range(NKV) for hh in range(4) for qt in range(2)]

        def emit_norm_rest(h, qt, pY, mY, vr, sc):
            pB = pmA[1]
            mB = mm_group(pB, [((lambda: ones_f32[:, :]), (lambda sc=sc: stat(sc)))],
                          waits=[vr])
            stat_rd[sc] = ("mm", mB)
            vc = P.emit("scalar", lambda s, pB=pB: s.copy(binv[:, 0:512], pB[:, :]),
                        waits=[("mm", mB), ("vv", wo_ready_vv), sg_cons[0], sg_cons[1]], inc=("ss", 1))
            psum_read(pB, "ss", vc)
            vy = P.emit("vector",
                        lambda v, h=h, qt=qt, pY=pY: v.tensor_tensor(
                            QYap(h, qt), pY[:, :], binv[:, 0:512], ALU.mult),
                        waits=[("ss", vc), ("mm", mY)], inc=("vv", 1))
            psum_read(pY, "vv", vy)
            return vy

        first_norm = True
        for it, (g, h, qt) in enumerate(iters):
            slope = SLOPES[g]
            q0 = HALO + 512 * qt
            ch = _chunks(g, qt)
            ch = [n for n in ch if n >= 4] + [n for n in ch if n < 4]
            n_t = len(ch)
            pY = pm[2 + it % 2]
            plX = plb[it % 2]
            yw0 = pst[id(pY)]["rd"]
            lw0 = pst[id(plX)]["rd"]
            sE_of = [None] * n_t

            def emit_S(ci, h=h, qt=qt, g=g, slope=slope, q0=q0, ch=ch, sE_of=sE_of):
                n = ch[ci]
                pS = SPS[ci % 3]
                slot = ci % 3
                sw = [("ss", q_evac[(h, qt)]), pst[id(pS)]["rd"]]
                if n < 4:
                    sw.append(("gb", halo_done))
                def fs(eng, g=g, n=n, h=h, qt=qt, pS=pS):
                    return eng.matmul(pS[:, :], Kap(g, n * 128, 128), QYap(h, qt),
                                      start=True, stop=True)
                mS = P.emit("tensor", fs, waits=sw, inc=("mm", 1))
                pst[id(pS)]["wr"] = ("mm", mS)
                a = n * 128 - q0
                sl_lo = 384 - a
                aw = [("mm", mS), ("sdma", LOADS)]
                def fa(v, sl_lo=sl_lo, slope=slope, pS=pS):
                    return v.scalar_tensor_tensor(pS[:, :], al_sb[:, sl_lo:sl_lo + 512],
                                                  slope, pS[:, :], ALU.mult, ALU.add)
                vA = P.emit("vector", fa, waits=aw, inc=("vv", 1))
                pst[id(pS)]["wr"] = ("vv", vA)
                ew = [("vv", vA), sp_cons[slot]]
                if n < 4:
                    bias_arg = hb_sb[:, g * 4 + n: g * 4 + n + 1]
                else:
                    bias_arg = None
                def fe(s, slot=slot, pS=pS, bias_arg=bias_arg):
                    if bias_arg is None:
                        return s.activation(Sap(slot), pS[:, :], AF.Exp)
                    return s.activation(Sap(slot), pS[:, :], AF.Exp, bias=bias_arg)
                sE = P.emit("scalar", fe, waits=ew, inc=("ss", 1))
                psum_read(pS, "ss", sE)
                sE_of[ci] = sE

            emit_S(0)
            emit_S(1)
            emit_S(2)
            if pending is not None:
                wo_ready_vv = emit_norm_rest(*pending)
                pending = None
            mY = mL = None
            for ci in range(n_t):
                n = ch[ci]
                slot = ci % 3
                first, last = ci == 0, ci == n_t - 1
                w_y = [("ss", sE_of[ci])] + ([yw0] if first and yw0 else [])
                if n < 4:
                    w_y.append(("gb", halo_done))
                def fy(eng, g=g, n=n, slot=slot, first=first, last=last, pY=pY):
                    return eng.matmul(pY[:, :], Vap(n, g * 128, 128), Sap(slot),
                                      start=first, stop=last)
                mY = P.emit("tensor", fy, waits=w_y, inc=(("mm", 1) if last else None))
                def fl(eng, slot=slot, first=first, last=last, plX=plX):
                    return eng.matmul(plX[0:1, :], ones_bf[:, :], Sap(slot),
                                      start=first, stop=last)
                mL = P.emit("tensor", fl, waits=([lw0] if first and lw0 else ()),
                            inc=("mm", 1))
                sp_cons[slot] = ("mm", mL)
                if ci + 3 < n_t:
                    emit_S(ci + 3)
            pst[id(pY)]["wr"] = ("mm", mY)
            pst[id(plX)]["wr"] = ("mm", mL)
            sc = it % 2
            vr_w = [("mm", mL), stat_rd[sc]]
            if first_norm:
                vr_w.append(last_stat_mm[0])
                first_norm = False
            P.emit("scalar", lambda s, plX=plX, sc=sc: s.activation(stat(sc), plX[0:1, :], AF.Ln),
                   waits=vr_w, inc=("ss", 1))
            vr_v = P.emit("scalar", lambda s, sc=sc: s.activation(stat(sc), stat(sc), AF.Exp, scale=-1.0),
                          inc=("ss", 1))
            vr = ("ss", vr_v)
            psum_read(plX, "ss", vr_v)
            pending = (h, qt, pY, mY, vr, sc)
        wo_ready_vv = emit_norm_rest(*pending)
        pending = None

        sg_cons[0] = ("vv", wo_ready_vv)
        sg_cons[1] = ("vv", wo_ready_vv)
        L3 = ln_start()
        ln3_pend = []
        wo_done_vv = None
        m_wo = None
        for o in range(NCH):
            par = o % 2
            wsem = "wg%d" % par
            def fdma(gp, o=o, par=par):
                return gp.dma_start(
                    out=wgu[:, par * 4096: par * 4096 + 2048].rearrange("p (hh oo) -> p hh oo", hh=NHEAD),
                    in_=wo[o])
            ldo = P.emit("gpsimd", fdma, waits=[slot_cons[par]], inc=(wsem, 16))
            wov = [None, None]
            for qt in range(2):
                pO = pm[(o * 2 + qt) % 2]
                tiles = [((lambda hh=hh, par=par: wgu[:, par * 4096 + hh * 128: par * 4096 + (hh + 1) * 128]),
                          (lambda hh=hh, qt=qt: QYap(hh, qt))) for hh in range(NHEAD)]
                mO = mm_group(pO, tiles, waits=[(wsem, ldo), ("vv", wo_ready_vv)])
                m_wo = mO
                wo_done_vv = P.emit("vector",
                                    lambda v, o=o, qt=qt, pO=pO: v.tensor_tensor(
                                        xv(o, qt * 512, 512), xv(o, qt * 512, 512), pO[:, :], ALU.add),
                                    waits=[("mm", mO)], inc=("vv", 1))
                psum_read(pO, "vv", wo_done_vv)
                wov[qt] = wo_done_vv
            slot_cons[par] = ("mm", m_wo)
            ln3_pend.append((o, wov[0], wov[1]))
            if len(ln3_pend) > 1:
                po, pv0, pv1 = ln3_pend.pop(0)
                for tau, pv in ((0, pv0), (1, pv1)):
                    cv = ln_cast(tau, po, [("vv", pv)])
                    ln_chunk(L3, tau, po, cv)
        for po, pv0, pv1 in ln3_pend:
            for tau, pv in ((0, pv0), (1, pv1)):
                cv = ln_cast(tau, po, [("vv", pv)])
                ln_chunk(L3, tau, po, cv)

        if stage == "attn":
            _emit_store(P, x_sb, out, [("vv", wo_done_vv)])
            _close(P, block, sems)
            return nc

        preload2 = {i: load_wgt(i, i) for i in range(4)}

        ln3_vv = ln_finale(L3, stat_guard=("mm", m_wo))
        ffn2, m_ffn2 = emit_ffn(ln3_vv, preload=preload2,
                                h_guard=("mm", m_wo), store=True)
        P.emit("sync", None, waits=[("sdma", P.cnt["sdma"])])
        _close(P, block, sems)
    return nc


def _emit_store(P, x_sb, out, waits):
    def fn(sy):
        return sy.dma_start(out=out, in_=x_sb[:, :].rearrange("p (c t) -> p c t", c=NCH))
    v = P.emit("sync", fn, waits=waits, inc=("sdma", 16))
    P.emit("sync", None, waits=[("sdma", v)])


def _close(P, block, sems):
    @block.tensor
    def _(t):
        P.replay("tensor", t, sems)

    @block.vector
    def _(v):
        P.replay("vector", v, sems)

    @block.scalar
    def _(s):
        P.replay("scalar", s, sems)

    @block.gpsimd
    def _(gp):
        P.replay("gpsimd", gp, sems)

    @block.sync
    def _(sy):
        P.replay("sync", sy, sems)



def _host_alibi():
    p = np.arange(128)[:, None]
    u = np.arange(1408)[None, :]
    v = (p - u + 384).astype(np.float32)
    v[p > u - 384] = -60000.0
    return v.astype(np.float16)


def _host_hb(core):
    hbv = np.full((16,), KILL if core == 0 else 0.0, dtype=np.float32)
    return np.ascontiguousarray(np.broadcast_to(hbv, (128, 16)))


def kernel(L0_attn_x, L0_attn_Wq, L0_attn_Wk, L0_attn_Wv, L0_attn_Wo,
           L0_ffn_w_gate_up, L0_ffn_w_down):
    x = np.asarray(L0_attn_x, dtype=np.float32)
    Wq = np.asarray(L0_attn_Wq, dtype=np.float32)
    Wk = np.asarray(L0_attn_Wk, dtype=np.float32)
    Wv = np.asarray(L0_attn_Wv, dtype=np.float32)
    Wo = np.asarray(L0_attn_Wo, dtype=np.float32)
    Wgu = np.asarray(L0_ffn_w_gate_up, dtype=np.float32)
    Wd = np.asarray(L0_ffn_w_down, dtype=np.float32)

    wgt = np.ascontiguousarray(Wgu.T.reshape(NCH, 128, 2, NI, 128).transpose(3, 1, 0, 2, 4))
    wdt = np.ascontiguousarray(Wd.T.reshape(NI, 128, NCH, 128).transpose(2, 1, 0, 3))
    wq = np.ascontiguousarray(Wq.reshape(NCH, 128, NHEAD, 128).transpose(2, 1, 0, 3))
    wk = np.ascontiguousarray(Wk.reshape(NCH, 128, NKV, 128).transpose(1, 0, 2, 3))
    wv = np.ascontiguousarray(Wv.reshape(NCH, 128, 512).transpose(1, 0, 2))
    wo = np.ascontiguousarray(Wo.reshape(NHEAD, 128, NCH, 128).transpose(2, 1, 0, 3))
    alibi = _host_alibi()

    nc = build(_STAGE)
    in_maps = []
    for c in range(NCORES):
        xs = np.ascontiguousarray(
            x[0, c * TPC:(c + 1) * TPC, :].T.reshape(NCH, 128, TPC).transpose(1, 0, 2))
        in_maps.append({
            "xt": xs, "wgt": wgt, "wdt": wdt, "wq": wq, "wk": wk, "wv": wv,
            "wo": wo, "alibi": alibi, "hb": _host_hb(c),
        })
    res = run_bass_kernel_spmd(nc, in_maps, core_ids=list(range(NCORES)))
    outp = np.empty((1, T, D), dtype=np.float32)
    for c in range(NCORES):
        o = np.asarray(res.results[c]["out"])
        outp[0, c * TPC:(c + 1) * TPC, :] = o.transpose(2, 1, 0).reshape(TPC, D)
    return outp


# version 15
# speedup vs baseline: 1.0480x; 1.0004x over previous
import math
import numpy as np
import ml_dtypes
import concourse.bass as bass
import concourse.mybir as mybir
from concourse.bass_utils import run_bass_kernel_spmd

F32 = mybir.dt.float32
BF16 = mybir.dt.bfloat16
FP16 = mybir.dt.float16
AF = mybir.ActivationFunctionType
ALU = mybir.AluOpType

NCORES = 8
T = 8192
D = 2048
TPC = T // NCORES
NCH = D // 128
INTER = 8192
NI = INTER // 128
NHEAD = 16
NKV = 4
HD = 128
HALO = 512
SLOPES = [2.0 ** (-0.5 * (g + 1)) for g in range(NKV)]
WINDOWS = [int(math.ceil(100.0 / s)) for s in SLOPES]
EPS = 1e-5
QSCALE = 1.0 / math.sqrt(HD)
KILL = -1e30

ENGS = ("tensor", "vector", "scalar", "gpsimd", "sync")
SEMS = ("wg0", "wg1", "wd0", "wd1", "gx", "gy", "gb", "sdma", "sdx0", "sdx1", "mm", "vv", "ss", "cc")
SLOT_SEM = ("wg0", "wg1", "wd0", "wd1")

_STAGE = "full"


def _chunks(g, qt):
    q0 = HALO + 512 * qt
    nlo = max(0, (q0 - WINDOWS[g]) // 128)
    nhi = q0 // 128 + 3
    return list(range(nlo, nhi + 1))


class Prog:

    def __init__(self):
        self.streams = {k: [] for k in ENGS}
        self.cnt = {s: 0 for s in SEMS}
        self.lastwait = {k: {} for k in ENGS}

    OWN = {"vector": "vv", "scalar": "ss"}

    def emit(self, eng, fn, waits=(), inc=None):
        wl = []
        own = self.OWN.get(eng)
        allw = list(waits)
        if own is not None:
            assert inc == (own, 1), (eng, inc)
            if self.cnt[own] > 0:
                allw.append((own, self.cnt[own]))
        for w in allw:
            if w is None:
                continue
            s, v = w
            if v and v > 0 and self.lastwait[eng].get(s, -1) < v:
                wl.append((s, v))
                self.lastwait[eng][s] = v
        val = None
        if inc is not None:
            self.cnt[inc[0]] += inc[1]
            val = self.cnt[inc[0]]
        self.streams[eng].append((tuple(wl), fn, inc))
        return val

    def replay(self, name, eng, sems):
        for wl, fn, inc in self.streams[name]:
            for s, v in wl:
                eng.wait_ge(sems[s], v)
            instr = fn(eng) if fn is not None else None
            if inc is not None:
                if instr is None:
                    instr = eng.nop()
                instr.then_inc(sems[inc[0]], inc[1])


def build(stage):
    nc = bass.Bass("TRN2", target_bir_lowering=False)

    xt = nc.dram_tensor("xt", [128, NCH, TPC], F32, kind="ExternalInput").ap()
    wgt = nc.dram_tensor("wgt", [NI, 128, NCH, 2, 128], F32, kind="ExternalInput").ap()
    wdt = nc.dram_tensor("wdt", [NCH, 128, NI, 128], F32, kind="ExternalInput").ap()
    wq = nc.dram_tensor("wq", [NHEAD, 128, NCH, 128], F32, kind="ExternalInput").ap()
    wk = nc.dram_tensor("wk", [128, NCH, NKV, 128], F32, kind="ExternalInput").ap()
    wv = nc.dram_tensor("wv", [128, NCH, 512], F32, kind="ExternalInput").ap()
    wo = nc.dram_tensor("wo", [NCH, 128, NHEAD, 128], F32, kind="ExternalInput").ap()
    alibi = nc.dram_tensor("alibi", [128, 1408], FP16, kind="ExternalInput").ap()
    hb = nc.dram_tensor("hb", [128, 16], F32, kind="ExternalInput").ap()
    out = nc.dram_tensor("out", [128, NCH, TPC], F32, kind="ExternalOutput").ap()
    cc_in = nc.dram_tensor("cc_in", [2, 512, 512], BF16).ap()
    cc_out = nc.dram_tensor("cc_out", [16, 512, 512], BF16).ap()

    from contextlib import ExitStack
    with ExitStack() as ctx:
        e = ctx.enter_context
        x_sb = e(nc.sbuf_tensor("x_sb", [128, NCH * TPC], F32))
        xn_sb = e(nc.sbuf_tensor("xn_sb", [128, NCH * TPC], BF16))
        uni = e(nc.sbuf_tensor("uni", [128, 32768], BF16))
        wgu = e(nc.sbuf_tensor("wgu", [128, 8192], BF16))
        wdb = e(nc.sbuf_tensor("wdb", [128, 8192], BF16))
        al_sb = e(nc.sbuf_tensor("al_sb", [128, 1408], FP16))
        hb_sb = e(nc.sbuf_tensor("hb_sb", [128, 16], F32))
        sgb = e(nc.sbuf_tensor("sgb", [128, 1024], BF16))
        stats = e(nc.sbuf_tensor("stats", [1, 2048], F32))
        tmp512 = e(nc.sbuf_tensor("tmp512", [128, 512], F32))
        ones_bf = e(nc.sbuf_tensor("ones_bf", [128, 1], BF16))
        ones_f32 = e(nc.sbuf_tensor("ones_f32", [1, 128], F32))

        pm = [e(nc.psum_tensor(f"pm{i}", [128, 512], F32)) for i in range(4)]
        pmA = [e(nc.psum_tensor(f"pmA{i}", [128, 512], F32)) for i in range(2)]
        plb = [e(nc.psum_tensor(f"pl{i}", [128, 512], F32)) for i in range(2)]
        PB = pm + pmA + plb

        sems = {s: e(nc.semaphore(s)) for s in SEMS}
        block = e(nc.Block())

        P = Prog()
        pst = {id(t): {"wr": None, "rd": None} for t in PB}

        def xnv(c, lo, n):
            return xn_sb[:, :].rearrange("p (c t) -> p c t", c=NCH)[:, c, lo:lo + n]

        def xv(c, lo, n):
            return x_sb[:, :].rearrange("p (c t) -> p c t", c=NCH)[:, c, lo:lo + n]

        def mm_group(ps, tiles, waits=(), rows=128, tile_waits=None):
            st = pst[id(ps)]
            w = list(waits)
            if st["rd"] is not None:
                w.append(st["rd"])
            n = len(tiles)
            val = None
            for k, (lf, rf) in enumerate(tiles):
                first, last = (k == 0), (k == n - 1)
                wk = list(w) if k == 0 else []
                if tile_waits is not None and tile_waits[k] is not None:
                    wk.append(tile_waits[k])
                def fn(eng, lf=lf, rf=rf, first=first, last=last, ps=ps, rows=rows):
                    return eng.matmul(ps[0:rows, :], lf(), rf(), start=first, stop=last)
                val = P.emit("tensor", fn, waits=wk,
                             inc=(("mm", 1) if last else None))
            st["wr"] = ("mm", val)
            return val

        def psum_read(ps, sem, val):
            pst[id(ps)]["rd"] = (sem, val)

        KOFF, VOFF, QYOFF, SPOFF = 0, NKV * 1536, 12288, 28672

        def Kap(kv, t0, n):
            return uni[:, KOFF + kv * 1536 + t0: KOFF + kv * 1536 + t0 + n]

        def Vap(n, d0, nd):
            return uni[:, VOFF + n * 512 + d0: VOFF + n * 512 + d0 + nd]

        def QYap(h, qt):
            o = QYOFF + h * 1024 + qt * 512
            return uni[:, o:o + 512]

        def Hap(tau, kk):
            o = (tau * 32 + kk) * 512
            return uni[:, o:o + 512]

        def Sap(slot):
            return uni[:, SPOFF + slot * 512: SPOFF + (slot + 1) * 512]

        def stat(i):
            return stats[:, i * 512:(i + 1) * 512]

        def slot_ap(slot, off, n):
            buf = wgu if slot < 2 else wdb
            base = (slot % 2) * 4096
            return buf[:, base + off: base + off + n]

        ones_rdy = P.emit("vector", lambda v: v.memset(ones_bf[:, :], 1.0), inc=("vv", 1))
        P.emit("vector", lambda v: v.memset(ones_f32[:, :], 1.0), inc=("vv", 1))
        for tau in range(2):
            def fn(sy, tau=tau):
                return sy.dma_start(
                    out=x_sb[:, :].rearrange("p (c t) -> p c t", c=NCH)[:, :, tau * 512:(tau + 1) * 512],
                    in_=xt[:, :, tau * 512:(tau + 1) * 512])
            P.emit("sync", fn, inc=("sdx%d" % tau, 16))
        P.emit("sync", lambda sy: sy.dma_start(out=al_sb[:, :], in_=alibi), inc=("sdma", 16))
        P.emit("sync", lambda sy: sy.dma_start(out=hb_sb[:, :], in_=hb), inc=("sdma", 16))
        LOADS = 32

        slot_cons = [None, None, None, None]
        sg_cons = [None, None]
        sp_cons = [None, None]
        last_stat_mm = [None]

        def load_wgt(i, slot):
            wsem = SLOT_SEM[slot]
            w = [slot_cons[slot]]
            def fdma(gp, i=i, slot=slot):
                return gp.dma_start(
                    out=slot_ap(slot, 0, 4096).rearrange(
                        "p (j g o) -> p j g o", j=NCH, g=2),
                    in_=wgt[i])
            return P.emit("gpsimd", fdma, waits=w, inc=(wsem, 16))

        def ln_start():
            return {"n": [0, 0], "mmv": {}, "vp": None, "sl": 0,
                    "pb": {(0, 'm'): PB[4], (0, 'q'): PB[5], (1, 'm'): PB[6], (1, 'q'): PB[7]}}

        def ln_cast(tau, c, waits):
            sem = "gx" if tau == 0 else "gy"
            w = list(waits)
            if P.cnt[sem] > 0:
                w.append((sem, P.cnt[sem]))
            def fc(gp, tau=tau, c=c):
                xr = x_sb[:, :].rearrange("p (c t) -> p c t", c=NCH)
                nr = xn_sb[:, :].rearrange("p (c t) -> p c t", c=NCH)
                if c is None:
                    return gp.dma_start(out=nr[:, :, tau * 512:(tau + 1) * 512],
                                        in_=xr[:, :, tau * 512:(tau + 1) * 512])
                return gp.dma_start(out=nr[:, c, tau * 512:(tau + 1) * 512],
                                    in_=xr[:, c, tau * 512:(tau + 1) * 512])
            return (sem, P.emit("gpsimd", fc, waits=w, inc=(sem, 16)))

        def ln_chunk(L, tau, c, cast_val):
            if L["vp"] is None:
                L["vp"] = [sg_cons[0], sg_cons[1]]
            pM, pQ = L["pb"][(tau, 'm')], L["pb"][(tau, 'q')]
            k = L["n"][tau]
            first, last = k == 0, k == NCH - 1
            wm = [cast_val, ("vv", ones_rdy)]
            if first:
                wm.append(pst[id(pM)]["rd"])
            def fm(eng, tau=tau, c=c, pM=pM, first=first, last=last):
                return eng.matmul(pM[0:1, :], ones_bf[:, :], xnv(c, tau * 512, 512),
                                  start=first, stop=last)
            mM = P.emit("tensor", fm, waits=wm, inc=("mm", 1))
            sl = L["sl"]
            L["sl"] ^= 1
            sq = P.emit("scalar",
                        lambda s, c=c, sl=sl, tau=tau: s.activation(
                            sgb[:, sl * 512:(sl + 1) * 512], xnv(c, tau * 512, 512), AF.Square),
                        waits=[cast_val, L["vp"][sl]], inc=("ss", 1))
            wq = [("ss", sq)]
            if first:
                wq.append(pst[id(pQ)]["rd"])
            def fq(eng, sl=sl, pQ=pQ, first=first, last=last):
                return eng.matmul(pQ[0:1, :], ones_bf[:, :], sgb[:, sl * 512:(sl + 1) * 512],
                                  start=first, stop=last)
            mQ = P.emit("tensor", fq, waits=wq, inc=("mm", 1))
            L["vp"][sl] = ("mm", mQ)
            L["n"][tau] += 1
            if last:
                L["mmv"][tau] = (mM, mQ)
                pst[id(pM)]["wr"] = ("mm", mM)
                pst[id(pQ)]["wr"] = ("mm", mQ)

        def ln_finale(L, stat_guard=None):
            inv_d = 1.0 / D
            binfo = {}
            for tau in range(2):
                so = 0
                pM, pQ = L["pb"][(tau, 'm')], L["pb"][(tau, 'q')]
                mM, mQ = L["mmv"][tau]
                sgw = [stat_guard] if stat_guard is not None else []
                if tau == 1:
                    sgw.append(("mm", binfo[0]))
                P.emit("vector", lambda v, pM=pM, so=so: v.tensor_scalar(
                    stat(so), pM[0:1, :], inv_d, None, ALU.mult),
                       waits=[("mm", mM)] + sgw, inc=("vv", 1))
                v2 = P.emit("vector", lambda v, pQ=pQ, so=so: v.tensor_scalar(
                    stat(so + 1), pQ[0:1, :], inv_d, EPS, ALU.mult, ALU.add),
                            waits=[("mm", mQ)], inc=("vv", 1))
                P.emit("vector", lambda v, so=so: v.tensor_tensor(
                    stat(so + 2), stat(so), stat(so), ALU.mult), inc=("vv", 1))
                v4 = P.emit("vector", lambda v, so=so: v.tensor_tensor(
                    stat(so + 1), stat(so + 1), stat(so + 2), ALU.subtract), inc=("vv", 1))
                s1 = P.emit("scalar", lambda s, so=so: s.activation(
                    stat(so + 2), stat(so + 1), AF.Sqrt), waits=[("vv", v4)], inc=("ss", 1))
                P.emit("vector", lambda v, so=so: v.reciprocal(stat(so + 3), stat(so + 2)),
                       waits=[("ss", s1)], inc=("vv", 1))
                P.emit("vector", lambda v, so=so: v.tensor_tensor(
                    stat(so + 2), stat(so + 3), stat(so + 3), ALU.mult), inc=("vv", 1))
                P.emit("vector", lambda v, so=so: v.tensor_tensor(
                    stat(so + 2), stat(so + 2), stat(so + 1), ALU.mult), inc=("vv", 1))
                P.emit("vector", lambda v, so=so: v.tensor_scalar(
                    stat(so + 2), stat(so + 2), -0.5, 1.5, ALU.mult, ALU.add), inc=("vv", 1))
                v9 = P.emit("vector", lambda v, so=so: v.tensor_tensor(
                    stat(so + 3), stat(so + 3), stat(so + 2), ALU.mult), inc=("vv", 1))
                vB = P.emit("vector", lambda v, so=so: v.scalar_tensor_tensor(
                    stat(so), stat(so), -1.0, stat(so + 3), ALU.mult, ALU.mult), inc=("vv", 1))
                psum_read(pM, "vv", vB)
                psum_read(pQ, "vv", v2)
                bA = mm_group(pM, [((lambda: ones_f32[:, :]), (lambda so=so: stat(so + 3)))],
                              waits=[("vv", v9)])
                bB = mm_group(pQ, [((lambda: ones_f32[:, :]), (lambda so=so: stat(so)))],
                              waits=[("vv", vB)])
                binfo[tau] = bB
                last_stat_mm[0] = ("mm", bB)
            xn_vv = [[None] * NCH, [None] * NCH]
            for c in range(NCH):
                for tau in range(2):
                    pM, pQ = L["pb"][(tau, 'm')], L["pb"][(tau, 'q')]
                    wts = [("mm", binfo[tau])] if c == 0 else ()
                    P.emit("vector",
                           lambda v, c=c, tau=tau, pM=pM: v.tensor_tensor(
                               tmp512[:, :], xv(c, tau * 512, 512), pM[:, :], ALU.mult),
                           waits=wts, inc=("vv", 1))
                    xn_vv[tau][c] = P.emit("vector",
                                           lambda v, c=c, tau=tau, pQ=pQ: v.tensor_tensor(
                                               xnv(c, tau * 512, 512), tmp512[:, :], pQ[:, :], ALU.add),
                                           inc=("vv", 1))
            for tau in range(2):
                psum_read(L["pb"][(tau, 'm')], "vv", xn_vv[tau][-1])
                psum_read(L["pb"][(tau, 'q')], "vv", xn_vv[tau][-1])
            sg_cons[0], sg_cons[1] = L["vp"][0], L["vp"][1]
            return xn_vv

        def emit_ffn(xn_vv, preload=None, h_guard=None, store=False, ln_job=None):
            vres = [None, None]
            m_down = None
            guard = h_guard
            ln_pend = []
            for half in range(2):
                hv_last = [None, None]
                for ii in range(32):
                    i = half * 32 + ii
                    slot = i % 4 if ii < 28 else i % 2
                    if preload is not None and i in preload:
                        ld = preload.pop(i)
                    else:
                        ld = load_wgt(i, slot)
                    wsem = SLOT_SEM[slot]
                    B = PB[0:4] if ii % 2 == 0 else PB[4:8]
                    grp = {}
                    for j in range(NCH):
                        for gi in range(2):
                            for tau in range(2):
                                ps = B[gi * 2 + tau]
                                first, last = j == 0, j == NCH - 1
                                w = [("vv", xn_vv[tau][j])]
                                if first:
                                    w += [(wsem, ld), pst[id(ps)]["rd"]]
                                def fn(eng, slot=slot, j=j, gi=gi, tau=tau, ps=ps,
                                       first=first, last=last):
                                    return eng.matmul(ps[:, :], slot_ap(slot, j * 256 + gi * 128, 128),
                                                      xnv(j, tau * 512, 512), start=first, stop=last)
                                val = P.emit("tensor", fn, waits=w,
                                             inc=(("mm", 1) if last else None))
                                if last:
                                    grp[(gi, tau)] = val
                                    pst[id(ps)]["wr"] = ("mm", val)
                    slot_cons[slot] = ("mm", val)
                    for tau in range(2):
                        sl = tau
                        sU = P.emit("scalar",
                                    lambda s, sl=sl, pG=B[tau]: s.activation(
                                        sgb[:, sl * 512:(sl + 1) * 512], pG[:, :], AF.Silu),
                                    waits=[("mm", grp[(0, tau)]), sg_cons[sl]], inc=("ss", 1))
                        psum_read(B[tau], "ss", sU)
                        hw = [("ss", sU), ("mm", grp[(1, tau)])]
                        if guard is not None:
                            hw.append(guard)
                        hv = P.emit("vector",
                                    lambda v, tau=tau, ii=ii, sl=sl, pU=B[2 + tau]: v.scalar_tensor_tensor(
                                        Hap(tau, ii), pU[:, :], 0.5, sgb[:, sl * 512:(sl + 1) * 512],
                                        ALU.mult, ALU.mult),
                                    waits=hw, inc=("vv", 1))
                        psum_read(B[2 + tau], "vv", hv)
                        sg_cons[sl] = ("vv", hv)
                        hv_last[tau] = hv
                for o in range(NCH):
                    wslot = 2 + o % 2
                    wsem = SLOT_SEM[wslot]
                    def fdma(gp, o=o, half=half, wslot=wslot):
                        return gp.dma_start(
                            out=slot_ap(wslot, 0, 4096).rearrange("p (q c) -> p q c", q=32),
                            in_=wdt[o, :, half * 32:(half + 1) * 32, :])
                    ld = P.emit("gpsimd", fdma, waits=[slot_cons[wslot]], inc=(wsem, 16))
                    pd = (PB[0], PB[1]) if o % 2 == 0 else (PB[2], PB[3])
                    md = [None, None]
                    for kk in range(32):
                        for tau in range(2):
                            ps = pd[tau]
                            first, last = kk == 0, kk == 31
                            w = []
                            if first:
                                w = [(wsem, ld), ("vv", hv_last[tau]), pst[id(ps)]["rd"]]
                            def fn(eng, wslot=wslot, kk=kk, tau=tau, ps=ps, first=first, last=last):
                                return eng.matmul(ps[:, :], slot_ap(wslot, kk * 128, 128),
                                                  Hap(tau, kk), start=first, stop=last)
                            md[tau] = P.emit("tensor", fn, waits=w,
                                             inc=(("mm", 1) if last else None))
                            if last:
                                pst[id(ps)]["wr"] = ("mm", md[tau])
                    slot_cons[wslot] = ("mm", md[1])
                    m_down = md[1]
                    for tau in range(2):
                        vres[tau] = P.emit("vector",
                                           lambda v, o=o, tau=tau, ps=pd[tau]: v.tensor_tensor(
                                               xv(o, tau * 512, 512), xv(o, tau * 512, 512), ps[:, :], ALU.add),
                                           waits=[("mm", md[tau])], inc=("vv", 1))
                        psum_read(pd[tau], "vv", vres[tau])
                        if store and half == 1:
                            def fst(sy, o=o, tau=tau):
                                return sy.dma_start(
                                    out=out[:, o, tau * 512:(tau + 1) * 512],
                                    in_=xv(o, tau * 512, 512))
                            P.emit("sync", fst, waits=[("vv", vres[tau])], inc=("sdma", 16))
                    if ln_job is not None and half == 1:
                        ln_pend.append((o, vres[0], vres[1]))
                        if len(ln_pend) > 1:
                            po, pv0, pv1 = ln_pend.pop(0)
                            for tau, pv in ((0, pv0), (1, pv1)):
                                cv = ln_cast(tau, po, [("vv", pv)])
                                ln_chunk(ln_job, tau, po, cv)
                guard = ("mm", m_down)
            if ln_job is not None:
                for po, pv0, pv1 in ln_pend:
                    for tau, pv in ((0, pv0), (1, pv1)):
                        cv = ln_cast(tau, po, [("vv", pv)])
                        ln_chunk(ln_job, tau, po, cv)
            return vres, m_down

        L1 = ln_start()
        cast0 = ln_cast(0, None, [("sdx0", 16)])
        cast1 = ln_cast(1, None, [("sdx1", 16)])
        preload1 = {i: load_wgt(i, i) for i in range(4)}
        for c in range(NCH):
            ln_chunk(L1, 0, c, cast0)
            ln_chunk(L1, 1, c, cast1)
        ln1_vv = ln_finale(L1)
        L2 = ln_start()
        ffn1, m_ffn1 = emit_ffn(ln1_vv, preload=preload1, ln_job=L2)

        if stage == "ffn1":
            _emit_store(P, x_sb, out, [("vv", ffn1[0]), ("vv", ffn1[1])])
            _close(P, block, sems)
            return nc

        ldk = P.emit("gpsimd",
                     lambda gp: gp.dma_start(
                         out=wgu[:, :].rearrange("p (j kv d) -> p j kv d", j=NCH, kv=NKV), in_=wk),
                     waits=[slot_cons[0], slot_cons[1]], inc=("wg0", 16))
        ldv = P.emit("gpsimd",
                     lambda gp: gp.dma_start(
                         out=wdb[:, :].rearrange("p (j d) -> p j d", j=NCH), in_=wv),
                     waits=[slot_cons[2], slot_cons[3]], inc=("wd0", 16))

        ln2_vv = ln_finale(L2, stat_guard=last_stat_mm[0])

        k_evac = None
        m_last = None
        for kv in range(NKV):
            for qt in range(2):
                pK = pm[(kv * 2 + qt) % 2]
                tiles = [((lambda j=j, kv=kv: wgu[:, j * 512 + kv * 128: j * 512 + (kv + 1) * 128]),
                          (lambda j=j, qt=qt: xnv(j, qt * 512, 512))) for j in range(NCH)]
                tw = [("vv", ln2_vv[qt][j]) for j in range(NCH)]
                mK = mm_group(pK, tiles, waits=[("wg0", ldk)], tile_waits=tw)
                m_last = mK
                k_evac = P.emit("scalar",
                                lambda s, kv=kv, qt=qt, pK=pK: s.activation(
                                    Kap(kv, HALO + qt * 512, 512), pK[:, :], AF.Copy),
                                waits=[("mm", mK)], inc=("ss", 1))
                psum_read(pK, "ss", k_evac)
        slot_cons[0] = slot_cons[1] = ("mm", m_last)
        v_evac = None
        for w8 in range(8):
            pV = pm[w8 % 2]
            tiles = [((lambda j=j, w8=w8: xnv(j, w8 * 128, 128)),
                      (lambda j=j: wdb[:, j * 512:(j + 1) * 512])) for j in range(NCH)]
            tw = [("vv", ln2_vv[w8 // 4][j]) for j in range(NCH)]
            mV = mm_group(pV, tiles, waits=[("wd0", ldv)], tile_waits=tw)
            m_last = mV
            v_evac = P.emit("scalar",
                            lambda s, w8=w8, pV=pV: s.activation(Vap(4 + w8, 0, 512), pV[:, :], AF.Copy),
                            waits=[("mm", mV)], inc=("ss", 1))
            psum_read(pV, "ss", v_evac)
        slot_cons[2] = slot_cons[3] = ("mm", m_last)

        def f_bk(gp):
            return gp.dma_start(
                out=cc_in[0].rearrange("(kv p) t -> p kv t", p=128),
                in_=uni[:, KOFF:KOFF + NKV * 1536].rearrange(
                    "p (kv t) -> p kv t", kv=NKV)[:, :, 1024:1536])
        P.emit("gpsimd", f_bk, waits=[("ss", k_evac)], inc=("gb", 16))
        def f_bv(gp):
            return gp.dma_start(
                out=cc_in[1].rearrange("(w p) d -> p w d", p=128),
                in_=uni[:, VOFF + 8 * 512: VOFF + 12 * 512].rearrange("p (w d) -> p w d", w=4))
        bv = P.emit("gpsimd", f_bv, waits=[("ss", v_evac)], inc=("gb", 16))

        ccv = P.emit("gpsimd",
                     lambda gp: gp.collective_compute(
                         "AllGather", ALU.bypass, replica_groups=[list(range(NCORES))],
                         ins=[cc_in], outs=[cc_out]),
                     waits=[("gb", bv)], inc=("cc", 1))

        q_evac = {}
        for h in range(NHEAD):
            par = h % 2
            wsem = "wg%d" % par
            def fdma(gp, h=h, par=par):
                return gp.dma_start(
                    out=wgu[:, par * 4096: par * 4096 + 2048].rearrange("p (j e) -> p j e", j=NCH),
                    in_=wq[h])
            ldq = P.emit("gpsimd", fdma, waits=[slot_cons[par]], inc=(wsem, 16))
            mq = None
            for qt in range(2):
                pQ = pm[2 + (h * 2 + qt) % 2]
                tiles = [((lambda j=j, par=par: wgu[:, par * 4096 + j * 128: par * 4096 + (j + 1) * 128]),
                          (lambda j=j, qt=qt: xnv(j, qt * 512, 512))) for j in range(NCH)]
                mq = mm_group(pQ, tiles, waits=[(wsem, ldq)],
                              tile_waits=[("vv", ln2_vv[qt][j]) for j in range(NCH)])
                q_evac[(h, qt)] = P.emit("scalar",
                                         lambda s, h=h, qt=qt, pQ=pQ: s.mul(QYap(h, qt), pQ[:, :], QSCALE),
                                         waits=[("mm", mq)], inc=("ss", 1))
                psum_read(pQ, "ss", q_evac[(h, qt)])
            slot_cons[par] = ("mm", mq)

        def f_halo(gp):
            cid = nc.partition_id([gp.engine])
            for i in gp.Switch(cid, NCORES):
                r = (i - 1) % NCORES
                gp.dma_start(
                    out=uni[:, KOFF:KOFF + NKV * 1536].rearrange(
                        "p (kv t) -> p kv t", kv=NKV)[:, :, 0:512],
                    in_=cc_out[2 * r].rearrange("(kv p) t -> p kv t", p=128)
                ).then_inc(sems["gb"], 16)
                gp.dma_start(
                    out=uni[:, VOFF:VOFF + 4 * 512].rearrange("p (w d) -> p w d", w=4),
                    in_=cc_out[2 * r + 1].rearrange("(w p) d -> p w d", p=128)
                ).then_inc(sems["gb"], 16)
        P.cnt["gb"] += 32
        halo_done = P.cnt["gb"]
        P.emit("gpsimd", lambda gp: (f_halo(gp), None)[1], waits=[("cc", ccv)])

        SPS = [pm[0], pm[1], pmA[0]]
        sp_cons = [("mm", m_ffn1)] * 3
        binv = sgb[:, :].bitcast(F32)
        wo_ready_vv = None
        stat_rd = [None, None]
        pending = None
        iters = [(g, g * 4 + hh, qt) for g in range(NKV) for hh in range(4) for qt in range(2)]

        def emit_norm_rest(h, qt, pY, mY, vr, sc):
            pB = pmA[1]
            mB = mm_group(pB, [((lambda: ones_f32[:, :]), (lambda sc=sc: stat(sc)))],
                          waits=[vr])
            stat_rd[sc] = ("mm", mB)
            vc = P.emit("scalar", lambda s, pB=pB: s.copy(binv[:, 0:512], pB[:, :]),
                        waits=[("mm", mB), ("vv", wo_ready_vv), sg_cons[0], sg_cons[1]], inc=("ss", 1))
            psum_read(pB, "ss", vc)
            vy = P.emit("vector",
                        lambda v, h=h, qt=qt, pY=pY: v.tensor_tensor(
                            QYap(h, qt), pY[:, :], binv[:, 0:512], ALU.mult),
                        waits=[("ss", vc), ("mm", mY)], inc=("vv", 1))
            psum_read(pY, "vv", vy)
            return vy

        first_norm = True
        for it, (g, h, qt) in enumerate(iters):
            slope = SLOPES[g]
            q0 = HALO + 512 * qt
            ch = _chunks(g, qt)
            ch = [n for n in ch if n >= 4] + [n for n in ch if n < 4]
            n_t = len(ch)
            pY = pm[2 + it % 2]
            plX = plb[it % 2]
            yw0 = pst[id(pY)]["rd"]
            lw0 = pst[id(plX)]["rd"]
            sE_of = [None] * n_t

            def emit_S(ci, h=h, qt=qt, g=g, slope=slope, q0=q0, ch=ch, sE_of=sE_of):
                n = ch[ci]
                pS = SPS[ci % 3]
                slot = ci % 3
                sw = [("ss", q_evac[(h, qt)]), pst[id(pS)]["rd"]]
                if n < 4:
                    sw.append(("gb", halo_done))
                def fs(eng, g=g, n=n, h=h, qt=qt, pS=pS):
                    return eng.matmul(pS[:, :], Kap(g, n * 128, 128), QYap(h, qt),
                                      start=True, stop=True)
                mS = P.emit("tensor", fs, waits=sw, inc=("mm", 1))
                pst[id(pS)]["wr"] = ("mm", mS)
                a = n * 128 - q0
                sl_lo = 384 - a
                aw = [("mm", mS), ("sdma", LOADS)]
                def fa(v, sl_lo=sl_lo, slope=slope, pS=pS):
                    return v.scalar_tensor_tensor(pS[:, :], al_sb[:, sl_lo:sl_lo + 512],
                                                  slope, pS[:, :], ALU.mult, ALU.add)
                vA = P.emit("vector", fa, waits=aw, inc=("vv", 1))
                pst[id(pS)]["wr"] = ("vv", vA)
                ew = [("vv", vA), sp_cons[slot]]
                if n < 4:
                    bias_arg = hb_sb[:, g * 4 + n: g * 4 + n + 1]
                else:
                    bias_arg = None
                def fe(s, slot=slot, pS=pS, bias_arg=bias_arg):
                    if bias_arg is None:
                        return s.activation(Sap(slot), pS[:, :], AF.Exp)
                    return s.activation(Sap(slot), pS[:, :], AF.Exp, bias=bias_arg)
                sE = P.emit("scalar", fe, waits=ew, inc=("ss", 1))
                psum_read(pS, "ss", sE)
                sE_of[ci] = sE

            emit_S(0)
            emit_S(1)
            emit_S(2)
            if pending is not None:
                wo_ready_vv = emit_norm_rest(*pending)
                pending = None
            mY = mL = None
            for ci in range(n_t):
                n = ch[ci]
                slot = ci % 3
                first, last = ci == 0, ci == n_t - 1
                w_y = [("ss", sE_of[ci])] + ([yw0] if first and yw0 else [])
                if n < 4:
                    w_y.append(("gb", halo_done))
                def fy(eng, g=g, n=n, slot=slot, first=first, last=last, pY=pY):
                    return eng.matmul(pY[:, :], Vap(n, g * 128, 128), Sap(slot),
                                      start=first, stop=last)
                mY = P.emit("tensor", fy, waits=w_y, inc=(("mm", 1) if last else None))
                def fl(eng, slot=slot, first=first, last=last, plX=plX):
                    return eng.matmul(plX[0:1, :], ones_bf[:, :], Sap(slot),
                                      start=first, stop=last)
                mL = P.emit("tensor", fl, waits=([lw0] if first and lw0 else ()),
                            inc=("mm", 1))
                sp_cons[slot] = ("mm", mL)
                if ci + 3 < n_t:
                    emit_S(ci + 3)
            pst[id(pY)]["wr"] = ("mm", mY)
            pst[id(plX)]["wr"] = ("mm", mL)
            sc = it % 2
            vr_w = [("mm", mL), stat_rd[sc]]
            if first_norm:
                vr_w.append(last_stat_mm[0])
                first_norm = False
            P.emit("scalar", lambda s, plX=plX, sc=sc: s.activation(stat(sc), plX[0:1, :], AF.Ln),
                   waits=vr_w, inc=("ss", 1))
            vr_v = P.emit("scalar", lambda s, sc=sc: s.activation(stat(sc), stat(sc), AF.Exp, scale=-1.0),
                          inc=("ss", 1))
            vr = ("ss", vr_v)
            psum_read(plX, "ss", vr_v)
            pending = (h, qt, pY, mY, vr, sc)
        wo_ready_vv = emit_norm_rest(*pending)
        pending = None

        sg_cons[0] = ("vv", wo_ready_vv)
        sg_cons[1] = ("vv", wo_ready_vv)
        L3 = ln_start()
        ln3_pend = []
        wo_done_vv = None
        m_wo = None
        for o in range(NCH):
            par = o % 2
            wsem = "wg%d" % par
            def fdma(gp, o=o, par=par):
                return gp.dma_start(
                    out=wgu[:, par * 4096: par * 4096 + 2048].rearrange("p (hh oo) -> p hh oo", hh=NHEAD),
                    in_=wo[o])
            ldo = P.emit("gpsimd", fdma, waits=[slot_cons[par]], inc=(wsem, 16))
            wov = [None, None]
            for qt in range(2):
                pO = pm[(o * 2 + qt) % 2]
                tiles = [((lambda hh=hh, par=par: wgu[:, par * 4096 + hh * 128: par * 4096 + (hh + 1) * 128]),
                          (lambda hh=hh, qt=qt: QYap(hh, qt))) for hh in range(NHEAD)]
                mO = mm_group(pO, tiles, waits=[(wsem, ldo), ("vv", wo_ready_vv)])
                m_wo = mO
                wo_done_vv = P.emit("vector",
                                    lambda v, o=o, qt=qt, pO=pO: v.tensor_tensor(
                                        xv(o, qt * 512, 512), xv(o, qt * 512, 512), pO[:, :], ALU.add),
                                    waits=[("mm", mO)], inc=("vv", 1))
                psum_read(pO, "vv", wo_done_vv)
                wov[qt] = wo_done_vv
            slot_cons[par] = ("mm", m_wo)
            ln3_pend.append((o, wov[0], wov[1]))
            if len(ln3_pend) > 1:
                po, pv0, pv1 = ln3_pend.pop(0)
                for tau, pv in ((0, pv0), (1, pv1)):
                    cv = ln_cast(tau, po, [("vv", pv)])
                    ln_chunk(L3, tau, po, cv)
        for po, pv0, pv1 in ln3_pend:
            for tau, pv in ((0, pv0), (1, pv1)):
                cv = ln_cast(tau, po, [("vv", pv)])
                ln_chunk(L3, tau, po, cv)

        if stage == "attn":
            _emit_store(P, x_sb, out, [("vv", wo_done_vv)])
            _close(P, block, sems)
            return nc

        preload2 = {i: load_wgt(i, i) for i in range(4)}

        ln3_vv = ln_finale(L3, stat_guard=("mm", m_wo))
        ffn2, m_ffn2 = emit_ffn(ln3_vv, preload=preload2,
                                h_guard=("mm", m_wo), store=True)
        P.emit("sync", None, waits=[("sdma", P.cnt["sdma"])])
        _close(P, block, sems)
    return nc


def _emit_store(P, x_sb, out, waits):
    def fn(sy):
        return sy.dma_start(out=out, in_=x_sb[:, :].rearrange("p (c t) -> p c t", c=NCH))
    v = P.emit("sync", fn, waits=waits, inc=("sdma", 16))
    P.emit("sync", None, waits=[("sdma", v)])


def _close(P, block, sems):
    @block.tensor
    def _(t):
        P.replay("tensor", t, sems)

    @block.vector
    def _(v):
        P.replay("vector", v, sems)

    @block.scalar
    def _(s):
        P.replay("scalar", s, sems)

    @block.gpsimd
    def _(gp):
        P.replay("gpsimd", gp, sems)

    @block.sync
    def _(sy):
        P.replay("sync", sy, sems)



def _host_alibi():
    p = np.arange(128)[:, None]
    u = np.arange(1408)[None, :]
    v = (p - u + 384).astype(np.float32)
    v[p > u - 384] = -60000.0
    return v.astype(np.float16)


def _host_hb(core):
    hbv = np.full((16,), KILL if core == 0 else 0.0, dtype=np.float32)
    return np.ascontiguousarray(np.broadcast_to(hbv, (128, 16)))


def kernel(L0_attn_x, L0_attn_Wq, L0_attn_Wk, L0_attn_Wv, L0_attn_Wo,
           L0_ffn_w_gate_up, L0_ffn_w_down):
    x = np.asarray(L0_attn_x, dtype=np.float32)
    Wq = np.asarray(L0_attn_Wq, dtype=np.float32)
    Wk = np.asarray(L0_attn_Wk, dtype=np.float32)
    Wv = np.asarray(L0_attn_Wv, dtype=np.float32)
    Wo = np.asarray(L0_attn_Wo, dtype=np.float32)
    Wgu = np.asarray(L0_ffn_w_gate_up, dtype=np.float32)
    Wd = np.asarray(L0_ffn_w_down, dtype=np.float32)

    wgt = np.ascontiguousarray(Wgu.T.reshape(NCH, 128, 2, NI, 128).transpose(3, 1, 0, 2, 4))
    wdt = np.ascontiguousarray(Wd.T.reshape(NI, 128, NCH, 128).transpose(2, 1, 0, 3))
    wq = np.ascontiguousarray(Wq.reshape(NCH, 128, NHEAD, 128).transpose(2, 1, 0, 3))
    wk = np.ascontiguousarray(Wk.reshape(NCH, 128, NKV, 128).transpose(1, 0, 2, 3))
    wv = np.ascontiguousarray(Wv.reshape(NCH, 128, 512).transpose(1, 0, 2))
    wo = np.ascontiguousarray(Wo.reshape(NHEAD, 128, NCH, 128).transpose(2, 1, 0, 3))
    alibi = _host_alibi()

    nc = build(_STAGE)
    in_maps = []
    for c in range(NCORES):
        xs = np.ascontiguousarray(
            x[0, c * TPC:(c + 1) * TPC, :].T.reshape(NCH, 128, TPC).transpose(1, 0, 2))
        in_maps.append({
            "xt": xs, "wgt": wgt, "wdt": wdt, "wq": wq, "wk": wk, "wv": wv,
            "wo": wo, "alibi": alibi, "hb": _host_hb(c),
        })
    res = run_bass_kernel_spmd(nc, in_maps, core_ids=list(range(NCORES)))
    outp = np.empty((1, T, D), dtype=np.float32)
    for c in range(NCORES):
        o = np.asarray(res.results[c]["out"])
        outp[0, c * TPC:(c + 1) * TPC, :] = o.transpose(2, 1, 0).reshape(TPC, D)
    return outp
